# version 16
# baseline (speedup 1.0000x reference)
import numpy as np
import ml_dtypes
from contextlib import ExitStack

import concourse.bass as bass
import concourse.mybir as mybir
from concourse.bass_utils import run_bass_kernel_spmd

F32 = mybir.dt.float32
BF16 = mybir.dt.bfloat16
I32 = mybir.dt.int32
AF = mybir.ActivationFunctionType
ALU = mybir.AluOpType
AX = mybir.AxisListType

NCORES = 8
T = 1024
D = 4096
NI = 11008
NIT = 86
EPS = 1e-5


def _p4_seq():
    seq, ra, sa = [], 0, 48
    for j in range(NIT):
        if (sa >= NIT) or (ra < 48 and (j * 48) // NIT >= ra):
            seq.append(ra); ra += 1
        else:
            seq.append(sa); sa += 1
    return seq


P4SEQ = _p4_seq()


def build_nc():
    nc = bass.Bass("TRN2", target_bir_lowering=False)

    d_x = nc.dram_tensor("x", [T, D], F32, kind="ExternalInput").ap()
    d_xT = nc.dram_tensor("xT", [D, T], F32, kind="ExternalInput").ap()
    d_hwt = nc.dram_tensor("hwt", [32, 128, 32, 128], F32, kind="ExternalInput").ap()
    d_mask = nc.dram_tensor("mask", [D, D], I32, kind="ExternalInput").ap()
    d_hwg = nc.dram_tensor("hwg", [NIT, 128, 32, 128], I32, kind="ExternalInput").ap()
    d_hwu = nc.dram_tensor("hwu", [NIT, 128, 32, 128], I32, kind="ExternalInput").ap()
    d_hwd = nc.dram_tensor("hwd", [NIT, 128, 8, 4, 128], I32, kind="ExternalInput").ap()
    d_sgv = nc.dram_tensor("sgv", [NIT, 128], F32, kind="ExternalInput").ap()
    d_suv = nc.dram_tensor("suv", [NIT, 128], F32, kind="ExternalInput").ap()
    d_sdv = nc.dram_tensor("sdv", [32, 128], F32, kind="ExternalInput").ap()
    d_e32 = nc.dram_tensor("e32f", [128, D], BF16, kind="ExternalInput").ap()
    d_gm4 = nc.dram_tensor("gmat4", [128, 4], F32, kind="ExternalInput").ap()
    d_idb = nc.dram_tensor("identb", [128, 128], BF16, kind="ExternalInput").ap()
    d_idf = nc.dram_tensor("identf", [128, 128], F32, kind="ExternalInput").ap()
    d_out = nc.dram_tensor("outT", [D, T], F32, kind="ExternalOutput").ap()
    d_hTd = nc.dram_tensor("hTd", [NIT, 128, T], BF16, kind="Internal").ap()
    d_sTd = nc.dram_tensor("sTd", [D, T], BF16, kind="Internal").ap()

    with ExitStack() as top:
        sem = lambda n: top.enter_context(nc.semaphore(n))
        sem2 = lambda n: [sem(n + "0"), sem(n + "1")]
        sem4 = lambda n: [sem(n + str(i)) for i in range(4)]
        sm_hT_st = sem2("hT_st")
        sm_sT_st = sem2("sT_st")
        sm_cst = [sem(f"cstp{i}") for i in range(5)]
        sm_mask = sem2("mask")
        sm_x = sem4("xld")
        p1v, p1s, p1t = sem("p1v"), sem("p1s"), sem("p1t")
        sm_wt = sem2("wt")
        sm_cst2 = sem("cst2")
        p2v, p2t = sem("p2v"), sem("p2t")
        sm_wg, sm_wu = sem2("wg"), sem2("wu")
        p3v, p3s, p3t = sem("p3v"), sem("p3s"), sem("p3t")
        sm_hT_ld, sm_wd = sem4("hT_ld"), sem4("wd")
        sm_hr = [sem(f"hr{i}") for i in range(6)]
        sm_xT, sm_sT_ld, sm_out = sem4("xT"), sem4("sT_ld"), sem4("out")
        p4v, p4s, p4t = sem("p4v"), sem("p4s"), sem("p4t")

        sb = lambda n, s, d: top.enter_context(nc.sbuf_tensor(n, s, d))
        sg_cols = sb("sg_cols", [128, NIT], F32)
        su_cols = sb("su_cols", [128, NIT], F32)
        sd_cols = sb("sd_cols", [128, 32], F32)
        wdb = sb("wdb", [128, 16 * 512], BF16)

        with ExitStack() as s123:
            sb3 = lambda n, s, d: s123.enter_context(nc.sbuf_tensor(n, s, d))
            xnT = sb3("xnT", [128, 32 * T], BF16)

            if True:
                with ExitStack() as s01:
                    sb1 = lambda n, s, d: s01.enter_context(nc.sbuf_tensor(n, s, d))
                    ps1 = lambda n, s, d: s01.enter_context(nc.psum_tensor(n, s, d))
                    idb_sb = sb1("idb_sb", [128, 128], BF16)
                    idf_sb = sb1("idf_sb", [128, 128], F32)
                    sgv_sb = sb1("sgv_sb", [NIT, 128], F32)
                    suv_sb = sb1("suv_sb", [NIT, 128], F32)
                    sdv_sb = sb1("sdv_sb", [32, 128], F32)
                    xs = [sb1(f"xs{i}", [128, D], F32) for i in range(4)]
                    xn = [sb1(f"xn{i}", [128, D], BF16) for i in range(4)]
                    st_sum = sb1("st_sum", [128, 8], F32)
                    st_ssq = sb1("st_ssq", [128, 8], F32)
                    st_mean = sb1("st_mean", [128, 8], F32)
                    st_msq = sb1("st_msq", [128, 8], F32)
                    st_m2 = sb1("st_m2", [128, 8], F32)
                    st_var = sb1("st_var", [128, 8], F32)
                    st_sqv = sb1("st_sqv", [128, 8], F32)
                    st_rst = sb1("st_rst", [128, 8], F32)
                    st_nmr = sb1("st_nmr", [128, 8], F32)
                    eps_col = sb1("eps_col", [128, 1], F32)
                    invd_col = sb1("invd_col", [128, 1], F32)
                    ninvd_col = sb1("ninvd_col", [128, 1], F32)
                    p_vec = [ps1(f"p_vec{i}", [128, 512], F32) for i in range(2)]
                    p_tr = [ps1(f"p_tr{i}", [128, 1024], BF16) for i in range(4)]

                    t_vtr = lambda k: k + 1
                    t_trb = lambda j, b: 3 + 4 * j + b + 1
                    s_sq = lambda j: j + 1 if j < 4 else j + 6
                    S_SQRT_A, S_SQRT_B = 5, 14
                    s_xn = lambda j: j + 6 if j < 4 else j + 11
                    V = {}

                    with nc.Block("prep") as blk:
                        @blk.vector
                        def _(ve):
                            vc = 0
                            def inc(ins):
                                nonlocal vc
                                ins.then_inc(p1v, 1)
                                vc += 1
                                return vc
                            inc(ve.memset(eps_col[:, :], EPS))
                            inc(ve.memset(invd_col[:, :], 1.0 / D))
                            inc(ve.memset(ninvd_col[:, :], -1.0 / D))
                            for k, dst in enumerate([(sg_cols, NIT), (su_cols, NIT),
                                                     (sd_cols, 32)]):
                                ve.wait_ge(p1t, t_vtr(k))
                                V[f"vecev{k}"] = inc(ve.tensor_copy(
                                    dst[0][:, 0:dst[1]], p_vec[k % 2][:, 0:dst[1]]))
                            def evac_tr(jj):
                                for b in range(4):
                                    ve.wait_ge(p1t, t_trb(jj, b))
                                    dst = xnT[:, :].rearrange(
                                        "p (kc t) -> p kc t", t=T)[
                                        :, b * 8:(b + 1) * 8,
                                        jj * 128:(jj + 1) * 128]
                                    src = p_tr[b][:, :].rearrange(
                                        "p (c t) -> p c t", t=128)
                                    V[f"trev{jj}_{b}"] = inc(ve.tensor_copy(dst, src))
                            def stats(h, sl, sqrt_lm):
                                n_mean = inc(ve.tensor_scalar(
                                    st_mean[:, sl], st_sum[:, sl],
                                    ninvd_col[:, 0:1], None, ALU.mult))
                                inc(ve.tensor_scalar(st_msq[:, sl], st_ssq[:, sl],
                                                     invd_col[:, 0:1], None, ALU.mult))
                                ve.wait_ge(p1v, n_mean)
                                n_m2 = inc(ve.tensor_tensor(
                                    st_m2[:, sl], st_mean[:, sl],
                                    st_mean[:, sl], ALU.mult))
                                ve.wait_ge(p1v, n_m2)
                                V[f"var_{h}"] = inc(ve.tensor_tensor(
                                    st_var[:, sl], st_msq[:, sl], st_m2[:, sl],
                                    ALU.subtract))
                                ve.wait_ge(p1s, sqrt_lm)
                                n_rst = inc(ve.reciprocal(st_rst[:, sl],
                                                          st_sqv[:, sl]))
                                ve.wait_ge(p1v, n_rst)
                                V[f"nmr_{h}"] = inc(ve.tensor_tensor(
                                    st_nmr[:, sl], st_mean[:, sl], st_rst[:, sl],
                                    ALU.mult))
                            for j in range(4):
                                ve.wait_ge(sm_x[j % 4], 16)
                                V[f"sum{j}"] = inc(ve.reduce_sum(
                                    st_sum[:, j:j + 1], xs[j % 4][:, :], AX.X))
                            ve.wait_ge(p1s, s_sq(3))
                            ve.wait_ge(p1v, V["sum3"])
                            stats("a", slice(0, 4), S_SQRT_A)
                            for j in range(4, 8):
                                ve.wait_ge(sm_x[j % 4], 32)
                                V[f"sum{j}"] = inc(ve.reduce_sum(
                                    st_sum[:, j:j + 1], xs[j % 4][:, :], AX.X))
                                if j >= 5:
                                    evac_tr(j - 5)
                            ve.wait_ge(p1s, s_sq(7))
                            ve.wait_ge(p1v, V["sum7"])
                            stats("b", slice(4, 8), S_SQRT_B)
                            for j in range(3, 8):
                                evac_tr(j)

                        @blk.scalar
                        def _(sc):
                            for i in (1, 3):
                                sc.dma_start(xs[i % 4][:, :],
                                             d_x[i * 128:(i + 1) * 128, :]
                                             ).then_inc(sm_x[i % 4], 16)

                            def half(js, var_lm, nmr_lm):
                                for j in js:
                                    sc.wait_ge(sm_x[j % 4], 16 * (j // 4 + 1))
                                    if j >= 4:
                                        sc.wait_ge(p1s, s_xn(j - 4))
                                        sc.wait_ge(p1t, t_trb(j - 4, 3))
                                    sc.activation(xn[j % 4][:, :], xs[j % 4][:, :],
                                                  AF.Square,
                                                  accum_out=st_ssq[:, j:j + 1]
                                                  ).then_inc(p1s, 1)
                                sc.wait_ge(p1v, V[var_lm])
                                sc.activation(st_sqv[:, js[0]:js[-1] + 1],
                                              st_var[:, js[0]:js[-1] + 1], AF.Sqrt,
                                              bias=eps_col[:, 0:1]).then_inc(p1s, 1)
                                sc.wait_ge(p1v, V[nmr_lm])
                                for j in js:
                                    sc.wait_ge(p1s, s_sq(j))
                                    sc.activation(xn[j % 4][:, :], xs[j % 4][:, :],
                                                  AF.Identity,
                                                  bias=st_nmr[:, j:j + 1],
                                                  scale=st_rst[:, j:j + 1]
                                                  ).then_inc(p1s, 1)
                            half([0, 1, 2, 3], "var_a", "nmr_a")
                            for i in (5, 7):
                                sc.wait_ge(p1s, s_xn(i - 4))
                                sc.dma_start(xs[i % 4][:, :],
                                             d_x[i * 128:(i + 1) * 128, :]
                                             ).then_inc(sm_x[i % 4], 16)
                            half([4, 5, 6, 7], "var_b", "nmr_b")

                        @blk.tensor
                        def _(te):
                            for i in range(5):
                                te.wait_ge(sm_cst[i], 16)
                            for k, (src, n) in enumerate([(sgv_sb, NIT), (suv_sb, NIT),
                                                          (sdv_sb, 32)]):
                                if k >= 2:
                                    te.wait_ge(p1v, V[f"vecev{k-2}"])
                                np_ = src.shape[0]
                                te.transpose(p_vec[k % 2][:, 0:np_], src[:, :],
                                             idf_sb[0:np_, 0:np_]).then_inc(p1t, 1)
                            for j in range(8):
                                te.wait_ge(p1s, s_xn(j))
                                for b in range(4):
                                    if j >= 1:
                                        te.wait_ge(p1v, V[f"trev{j-1}_{b}"])
                                    for c in range(8):
                                        kc = b * 8 + c
                                        ins = te.transpose(
                                            p_tr[b][:, c * 128:(c + 1) * 128],
                                            xn[j % 4][:, kc * 128:(kc + 1) * 128],
                                            idb_sb[:, :])
                                        if c == 7:
                                            ins.then_inc(p1t, 1)

                        @blk.sync
                        def _(sy):
                            for i in (0, 2):
                                sy.dma_start(xs[i % 4][:, :],
                                             d_x[i * 128:(i + 1) * 128, :]
                                             ).then_inc(sm_x[i % 4], 16)
                            for ci, (dst, src) in enumerate(
                                    [(idb_sb[:, :], d_idb), (idf_sb[:, :], d_idf),
                                     (sgv_sb[:, :], d_sgv), (suv_sb[:, :], d_suv),
                                     (sdv_sb[:, :], d_sdv)]):
                                sy.dma_start(dst, src).then_inc(sm_cst[ci], 16)
                            for i in (4, 6):
                                sy.wait_ge(p1s, s_xn(i - 4))
                                sy.dma_start(xs[i % 4][:, :],
                                             d_x[i * 128:(i + 1) * 128, :]
                                             ).then_inc(sm_x[i % 4], 16)

                with ExitStack() as s2:
                    sb2l = lambda n, s, d: s2.enter_context(nc.sbuf_tensor(n, s, d))
                    ps2 = lambda n, s, d: s2.enter_context(nc.psum_tensor(n, s, d))
                    e32_sb = sb2l("e32_sb", [128, D], BF16)
                    gm4_sb = sb2l("gm4_sb", [128, 4], F32)
                    idb2_sb = sb2l("idb2_sb", [128, 128], BF16)
                    mk = [sb2l(f"mk{i}", [128, D], I32) for i in range(2)]
                    red = [sb2l(f"red{i}", [128, 128], F32) for i in range(2)]
                    bm01t = [sb2l(f"bm01t{i}", [4, 128], BF16) for i in range(2)]
                    bmfeT = sb2l("bmfeT", [128, 128], BF16)
                    wts = [sb2l(f"wts{i}", [128, D], F32) for i in range(2)]
                    wtm = [sb2l(f"wtm{i}", [128, D], BF16) for i in range(2)]
                    sTs = [sb2l(f"sTs{i}", [128, T], BF16) for i in range(2)]
                    p_me4 = ps2("p_me4", [128, 1024], F32)
                    p_sT = [[ps2(f"p_sT{i}_{h}", [128, 512], F32) for h in range(2)]
                            for i in range(2)]
                    p_bm = ps2("p_bm", [128, 512], F32)
                    p_fe = ps2("p_fe", [128, 1024], BF16)

                    NK = 46
                    TL, VL = {}, {}
                    tc = vc = 0
                    for k in range(NK):
                        if 0 <= k - 1 <= 31:
                            tc += 1; TL[("bm4", k - 1)] = tc
                        if 0 <= k - 2 <= 31:
                            tc += 1; TL[("tr4", k - 2)] = tc
                        if k - 3 >= 0 and (k - 3) % 8 == 7 and (k - 3) // 8 <= 3:
                            tc += 32; TL[("exp", (k - 3) // 8)] = tc
                        if 0 <= k - 11 <= 31:
                            tc += 1; TL[("main", k - 11)] = tc
                    for k in range(NK):
                        if k <= 31:
                            vc += 1; VL[("red", k)] = vc
                        if 0 <= k - 1 <= 31:
                            vc += 1; VL[("thresh", k - 1)] = vc
                        if 0 <= k - 2 <= 31:
                            vc += 1; VL[("feev", k - 2)] = vc
                        if 0 <= k - 10 <= 31:
                            vc += 1; VL[("mult", k - 10)] = vc
                        if 0 <= k - 12 <= 31:
                            vc += 2; VL[("evac", k - 12)] = vc

                    with nc.Block("attn") as blk:
                        @blk.sync
                        def _(sy):
                            for ci, (dst, srcd) in enumerate(
                                    [(e32_sb[:, :], d_e32), (gm4_sb[:, :], d_gm4),
                                     (idb2_sb[:, :], d_idb)]):
                                sy.dma_start(dst, srcd).then_inc(sm_cst2, 16)
                                sy.wait_ge(sm_cst2, 16 * (ci + 1))

                            def m_issue(k):
                                if k >= 2:
                                    sy.wait_ge(p2v, VL[("red", k - 2)])
                                sy.dma_start(mk[k % 2][:, :],
                                             d_mask[k * 128:(k + 1) * 128, :]
                                             ).then_inc(sm_mask[k % 2], 16)
                            for k in range(10):
                                m_issue(k)
                            for k in range(32):
                                if k + 10 <= 31:
                                    m_issue(k + 10)
                                if k >= 2:
                                    sy.wait_ge(p2v, VL[("mult", k - 2)])
                                sy.dma_start(
                                    wts[k % 2][:, :].rearrange(
                                        "p (kb n) -> p kb n", n=128),
                                    d_hwt[k]).then_inc(sm_wt[k % 2], 16)

                        @blk.tensor
                        def _(te):
                            def bm4(k):
                                te.wait_ge(p2v, VL[("red", k)])
                                if k >= 1:
                                    te.wait_ge(p2v, VL[("thresh", k - 1)])
                                te.matmul(p_bm[0:4, 0:128], gm4_sb[:, :],
                                          red[k % 2][:, :], start=True,
                                          stop=True).then_inc(p2t, 1)

                            def tr4(k):
                                te.wait_ge(p2v, VL[("thresh", k)])
                                if k >= 1:
                                    te.wait_ge(p2v, VL[("feev", k - 1)])
                                te.transpose(p_fe[:, 0:4], bm01t[k % 2][:, :],
                                             idb2_sb[0:4, 0:4]).then_inc(p2t, 1)

                            def exp(G):
                                te.wait_ge(p2v, VL[("feev", 8 * G + 7)])
                                if G >= 1:
                                    te.wait_ge(p2v, VL[("mult", 8 * G - 1)])
                                for kt in range(32):
                                    ins = te.matmul(
                                        p_me4[:, kt * 32:kt * 32 + 32],
                                        e32_sb[:, kt * 128:(kt + 1) * 128],
                                        bmfeT[:, G * 32:(G + 1) * 32],
                                        start=True, stop=True)
                                ins.then_inc(p2t, 32)

                            def main(nt):
                                par = nt % 2
                                te.wait_ge(p2v, VL[("mult", nt)])
                                if nt >= 2:
                                    te.wait_ge(p2v, VL[("evac", nt - 2)])
                                for kb in range(32):
                                    A = wtm[par][:, kb * 128:(kb + 1) * 128]
                                    for th in range(2):
                                        ins = te.matmul(
                                            p_sT[par][th][:, :], A,
                                            xnT[:, kb * T + th * 512:
                                                kb * T + th * 512 + 512],
                                            start=(kb == 0), stop=(kb == 31))
                                        if kb == 31 and th == 1:
                                            ins.then_inc(p2t, 1)
                            te.wait_ge(sm_cst2, 48)
                            for k in range(NK):
                                if 0 <= k - 1 <= 31:
                                    bm4(k - 1)
                                if 0 <= k - 2 <= 31:
                                    tr4(k - 2)
                                if k - 3 >= 0 and (k - 3) % 8 == 7 and (k - 3) // 8 <= 3:
                                    exp((k - 3) // 8)
                                if 0 <= k - 11 <= 31:
                                    main(k - 11)

                        @blk.vector
                        def _(ve):
                            def redf(k):
                                ve.wait_ge(sm_mask[k % 2], 16 * (k // 2 + 1))
                                if k >= 2:
                                    ve.wait_ge(p2t, TL[("bm4", k - 2)])
                                ve.tensor_reduce(
                                    red[k % 2][:, :],
                                    mk[k % 2][:, :].rearrange("p (s e) -> p s e", e=32),
                                    AX.X, ALU.add).then_inc(p2v, 1)

                            def thresh(k):
                                ve.wait_ge(p2t, TL[("bm4", k)])
                                if k >= 2:
                                    ve.wait_ge(p2t, TL[("tr4", k - 2)])
                                ve.tensor_scalar(bm01t[k % 2][:, :], p_bm[0:4, 0:128],
                                                 0.0, None, ALU.is_gt).then_inc(p2v, 1)

                            def feev(k):
                                ve.wait_ge(p2t, TL[("tr4", k)])
                                ve.tensor_copy(bmfeT[:, 4 * k:4 * k + 4],
                                               p_fe[:, 0:4]).then_inc(p2v, 1)

                            def mult(nt):
                                par = nt % 2
                                G = nt // 8
                                ve.wait_ge(sm_wt[nt % 2], 16 * (nt // 2 + 1))
                                ve.wait_ge(p2t, TL[("exp", G)])
                                if nt >= 2:
                                    ve.wait_ge(p2t, TL[("main", nt - 2)])
                                ap1 = bass.AP(p_me4.ap().tensor,
                                              (nt % 8) * 4,
                                              [[1024, 128], [32, 32], [1, 4], [0, 32]])
                                ve.tensor_tensor(
                                    wtm[par][:, :].rearrange(
                                        "p (a b c) -> p a b c", a=32, b=4),
                                    wts[par][:, :].rearrange(
                                        "p (a b c) -> p a b c", a=32, b=4),
                                    ap1, ALU.mult).then_inc(p2v, 1)

                            def evac(nt):
                                par = nt % 2
                                ve.wait_ge(p2t, TL[("main", nt)])
                                if nt >= 2:
                                    ve.wait_ge(sm_sT_st[nt % 2], 16 * (nt // 2))
                                for th in range(2):
                                    ve.tensor_copy(
                                        sTs[par][:, th * 512:th * 512 + 512],
                                        p_sT[par][th][:, :]).then_inc(p2v, 1)
                            for k in range(NK):
                                if k <= 31:
                                    redf(k)
                                if 0 <= k - 1 <= 31:
                                    thresh(k - 1)
                                if 0 <= k - 2 <= 31:
                                    feev(k - 2)
                                if 0 <= k - 10 <= 31:
                                    mult(k - 10)
                                if 0 <= k - 12 <= 31:
                                    evac(k - 12)

                        @blk.scalar
                        def _(sc):
                            for nt in range(32):
                                sc.wait_ge(p2v, VL[("evac", nt)])
                                sc.dma_start(d_sTd[nt * 128:(nt + 1) * 128, :],
                                             sTs[nt % 2][:, :]
                                             ).then_inc(sm_sT_st[nt % 2], 16)

            with ExitStack() as s3:
                sb3l = lambda n, s, d: s3.enter_context(nc.sbuf_tensor(n, s, d))
                ps3 = lambda n, s, d: s3.enter_context(nc.psum_tensor(n, s, d))
                wgb = [sb3l(f"wgb{i}", [128, D], BF16) for i in range(2)]
                wub = [sb3l(f"wub{i}", [128, D], BF16) for i in range(2)]
                sil = [sb3l(f"sil{i}", [128, T], F32) for i in range(2)]
                uss = [sb3l(f"uss{i}", [128, T], F32) for i in range(2)]
                hTt = [sb3l(f"hTt{i}", [128, T], BF16) for i in range(2)]
                p_g = [[ps3(f"p_g{i}_{h}", [128, 512], F32) for h in range(2)]
                       for i in range(2)]
                p_u = [[ps3(f"p_u{i}_{h}", [128, 512], F32) for h in range(2)]
                       for i in range(2)]

                with nc.Block("ffnA") as blk:
                    @blk.gpsimd
                    def _(gp):
                        for it in range(NIT):
                            if it >= 2:
                                gp.wait_ge(p3t, it - 1)
                            gp.dma_start(
                                wgb[it % 2][:, :].rearrange("p (b i) -> p b i", i=128),
                                d_hwg[it]).then_inc(sm_wg[it % 2], 16)
                            gp.dma_start(
                                wub[it % 2][:, :].rearrange("p (b i) -> p b i", i=128),
                                d_hwu[it]).then_inc(sm_wu[it % 2], 16)
                        for gb in range(2):
                            gp.dma_start(
                                wdb[:, gb * 2048:gb * 2048 + 2048].rearrange(
                                    "p (i hh) -> p i hh", i=4),
                                d_hwd[gb * 4:gb * 4 + 4, :, 0, :, :].rearrange(
                                    "i p a b -> p i (a b)")
                                ).then_inc(sm_wd[gb], 16)

                    @blk.tensor
                    def _(te):
                        for it in range(NIT):
                            par = it % 2
                            te.wait_ge(sm_wg[it % 2], 16 * (it // 2 + 1))
                            te.wait_ge(sm_wu[it % 2], 16 * (it // 2 + 1))
                            if it >= 2:
                                te.wait_ge(p3s, 2 * (it - 2) + 2)
                                te.wait_ge(p3v, 3 * (it - 2) + 2)
                            for w_b, p_x in [(wgb[par], p_g[par]), (wub[par], p_u[par])]:
                                for blk_h in range(32):
                                    A = w_b[:, blk_h * 128:(blk_h + 1) * 128]
                                    for th in range(2):
                                        ins = te.matmul(
                                            p_x[th][:, :], A,
                                            xnT[:, blk_h * T + th * 512:
                                                blk_h * T + th * 512 + 512],
                                            start=(blk_h == 0), stop=(blk_h == 31))
                            ins.then_inc(p3t, 1)

                    @blk.scalar
                    def _(sc):
                        for it in range(NIT):
                            par = it % 2
                            if it >= 1:
                                sc.wait_ge(p3v, 3 * (it - 1) + 3)
                                sc.dma_start(d_hTd[it - 1], hTt[(it - 1) % 2][:, :]
                                             ).then_inc(sm_hT_st[(it - 1) % 2], 16)
                            sc.wait_ge(p3t, it + 1)
                            if it >= 2:
                                sc.wait_ge(p3v, 3 * (it - 2) + 3)
                            for th in range(2):
                                sc.activation(sil[par][:, th * 512:th * 512 + 512],
                                              p_g[par][th][:, :], AF.Silu,
                                              scale=sg_cols[:, it:it + 1]
                                              ).then_inc(p3s, 1)
                        sc.wait_ge(p3v, 3 * (NIT - 1) + 3)
                        sc.dma_start(d_hTd[NIT - 1], hTt[(NIT - 1) % 2][:, :]
                                     ).then_inc(sm_hT_st[(NIT - 1) % 2], 16)

                    @blk.vector
                    def _(ve):
                        for it in range(NIT):
                            par = it % 2
                            ve.wait_ge(p3t, it + 1)
                            if it >= 2:
                                ve.wait_ge(p3v, 3 * it - 3)
                            for th in range(2):
                                ve.tensor_scalar(uss[par][:, th * 512:th * 512 + 512],
                                                 p_u[par][th][:, :],
                                                 su_cols[:, it:it + 1], None,
                                                 ALU.mult).then_inc(p3v, 1)
                            ve.wait_ge(p3s, 2 * it + 2)
                            if it >= 2:
                                ve.wait_ge(sm_hT_st[it % 2], 16 * (it // 2))
                            ve.wait_ge(p3v, 3 * it + 2)
                            ve.tensor_tensor(hTt[par][:, :], sil[par][:, :],
                                             uss[par][:, :], ALU.mult).then_inc(p3v, 1)

        with ExitStack() as s4:
            sb4 = lambda n, s, d: s4.enter_context(nc.sbuf_tensor(n, s, d))
            ps4 = lambda n, s, d: s4.enter_context(nc.psum_tensor(n, s, d))
            hTr = sb4("hTr", [128, 48 * 1024], BF16)
            hTl = [sb4(f"hTl{i}", [128, 2048], BF16) for i in range(4)]
            gTs = [sb4(f"gTs{i}", [128, T], F32) for i in range(4)]
            sTl = [sb4(f"sTl{i}", [128, T], F32) for i in range(4)]
            xTl = [sb4(f"xTl{i}", [128, T], F32) for i in range(4)]
            outs = [sb4(f"outs{i}", [128, T], F32) for i in range(4)]
            tmpm = sb4("tmpm", [128, T], F32)
            p_acc = [[ps4(f"p_acc{h}_{t}", [128, 512], F32) for t in range(2)]
                     for h in range(4)]

            NB = 22
            NPAIR = 19
            SEQ = P4SEQ
            assert sorted(SEQ) == list(range(NIT))
            POS = {it: j for j, it in enumerate(SEQ)}
            ht_use = {}
            _cnt4 = [0, 0, 0, 0]
            for _g in range(8):
                for _p in range(NPAIR):
                    _cnt4[_p % 4] += 1
                    ht_use[(_g, _p)] = _cnt4[_p % 4]
            wd_use = {}
            _cnt4w = [0, 0, 0, 0]
            for _g in range(8):
                for _b in range(NB):
                    _gb = _g * NB + _b
                    _cnt4w[_gb % 4] += 1
                    wd_use[(_g, _b)] = _cnt4w[_gb % 4]

            with nc.Block("down") as blk:
                @blk.sync
                def _(sy):
                    for r in range(6):
                        sy.wait_ge(sm_hT_st[0], 16 * (4 * r + 4))
                        sy.wait_ge(sm_hT_st[1], 16 * (4 * r + 4))
                        sy.dma_start(
                            hTr[:, r * 8192:(r + 1) * 8192].rearrange(
                                "q (i t) -> q i t", i=8),
                            d_hTd[8 * r:8 * r + 8, :, :].rearrange(
                                "i q t -> q i t")).then_inc(sm_hr[r], 16)
                    sy.wait_ge(sm_hT_st[0], 16 * 43)
                    sy.wait_ge(sm_hT_st[1], 16 * 43)
                    sy.wait_ge(sm_sT_st[0], 16 * 16)
                    sy.wait_ge(sm_sT_st[1], 16 * 16)
                    for g in range(8):
                        for p in range(NPAIR):
                            if p >= 4:
                                sy.wait_ge(p4t,
                                           g * NIT + POS[48 + 2 * (p - 4) + 1] + 1)
                            elif g >= 1:
                                q = {0: 16, 1: 17, 2: 18, 3: 15}[p]
                                sy.wait_ge(p4t,
                                           (g - 1) * NIT + POS[48 + 2 * q + 1] + 1)
                            sy.dma_start(
                                hTl[p % 4][:, :].rearrange(
                                    "q (i t) -> q i t", i=2),
                                d_hTd[48 + 2 * p:48 + 2 * p + 2, :, :].rearrange(
                                    "i q t -> q i t")).then_inc(sm_hT_ld[p % 4], 16)
                            if p == 2:
                                for ht in range(4):
                                    hg = g * 4 + ht
                                    if g >= 1:
                                        sy.wait_ge(p4v, 8 * (g - 1) + 2 * (ht + 1))
                                    sy.dma_start(xTl[ht][:, :],
                                                 d_xT[hg * 128:(hg + 1) * 128, :]
                                                 ).then_inc(sm_xT[ht], 16)
                    for ht in range(4):
                        sy.wait_ge(sm_out[ht], 16 * 8)

                @blk.gpsimd
                def _(gp):
                    for g in range(8):
                        for b in range(NB):
                            it0 = b * 4
                            nit = min(4, NIT - it0)
                            gb = g * NB + b
                            if gb < 2:
                                continue
                            if gb >= 4:
                                pgb = gb - 4
                                pg, pb = pgb // NB, pgb % NB
                                gp.wait_ge(p4t,
                                           pg * NIT + min(4 * (pb + 1), NIT))
                            sl = (gb % 4) * 2048
                            dst = wdb[:, sl:sl + nit * 512]
                            gp.dma_start(
                                dst.rearrange("p (i hh) -> p i hh", i=nit),
                                d_hwd[it0:it0 + nit, :, g, :, :].rearrange(
                                    "i p a b -> p i (a b)")
                                ).then_inc(sm_wd[gb % 4], 16)
                        for ht in range(4):
                            hg = g * 4 + ht
                            if g >= 1:
                                gp.wait_ge(p4v, 8 * (g - 1) + 2 * (ht + 1))
                            gp.wait_ge(sm_sT_st[hg % 2], 16 * (hg // 2 + 1))
                            gp.dma_start(sTl[ht][:, :],
                                         d_sTd[hg * 128:(hg + 1) * 128, :]
                                         ).then_inc(sm_sT_ld[ht], 16)

                @blk.tensor
                def _(te):
                    for g in range(8):
                        for j in range(NIT):
                            it = SEQ[j]
                            b = j // 4
                            gb = g * NB + b
                            te.wait_ge(sm_wd[gb % 4], 16 * wd_use[(g, b)])
                            if it < 48:
                                if g == 0:
                                    te.wait_ge(sm_hr[it // 8], 16)
                                slot, sbase = hTr, it * 1024
                            else:
                                pr = (it - 48) // 2
                                te.wait_ge(sm_hT_ld[pr % 4],
                                           16 * ht_use[(g, pr)])
                                slot, sbase = hTl[pr % 4], (it % 2) * 1024
                            base = (gb % 4) * 2048 + (j % 4) * 512
                            for ht in range(4):
                                A = wdb[:, base + ht * 128:base + ht * 128 + 128]
                                for th in range(2):
                                    if j == 0 and g >= 1:
                                        te.wait_ge(p4s,
                                                   8 * (g - 1) + 2 * ht + th + 1)
                                    B = slot[:, sbase + th * 512:
                                             sbase + th * 512 + 512]
                                    ins = te.matmul(p_acc[ht][th][:, :], A, B,
                                                    start=(j == 0), stop=(j == NIT - 1))
                            ins.then_inc(p4t, 1)

                @blk.scalar
                def _(sc):
                    for g in range(8):
                        sc.wait_ge(p4t, g * NIT + NIT)
                        for ht in range(4):
                            if g >= 1:
                                sc.wait_ge(p4v, 8 * (g - 1) + 2 * (ht + 1))
                            for th in range(2):
                                sc.activation(gTs[ht][:, th * 512:th * 512 + 512],
                                              p_acc[ht][th][:, :], AF.Silu,
                                              scale=sd_cols[:, g * 4 + ht:g * 4 + ht + 1]
                                              ).then_inc(p4s, 1)
                        for ht in range(4):
                            hg = g * 4 + ht
                            sc.wait_ge(p4v, 8 * g + 2 * (ht + 1))
                            sc.dma_start(d_out[hg * 128:(hg + 1) * 128, :],
                                         outs[ht][:, :]).then_inc(sm_out[ht], 16)

                @blk.vector
                def _(ve):
                    for g in range(8):
                        for ht in range(4):
                            k = 4 * g + ht
                            ve.wait_ge(p4s, 8 * g + 2 * (ht + 1))
                            ve.wait_ge(sm_xT[ht], 16 * (g + 1))
                            ve.wait_ge(sm_sT_ld[ht], 16 * (g + 1))
                            if g >= 1:
                                ve.wait_ge(sm_out[ht], 16 * g)
                            if k >= 1:
                                ve.wait_ge(p4v, 2 * k)
                            ve.tensor_tensor(tmpm[:, :], gTs[ht][:, :], sTl[ht][:, :],
                                             ALU.mult).then_inc(p4v, 1)
                            ve.wait_ge(p4v, 2 * k + 1)
                            ve.tensor_tensor(outs[ht][:, :], tmpm[:, :], xTl[ht][:, :],
                                             ALU.add).then_inc(p4v, 1)

    nc.finalize()
    return nc


_NC_CACHE = None


def kernel(L0_attn_data, L0_attn_weight, L0_attn_mask, L0_ffn_w_gate, L0_ffn_s_gate,
           L0_ffn_w_up, L0_ffn_s_up, L0_ffn_w_down, L0_ffn_s_down):
    global _NC_CACHE
    asc = np.ascontiguousarray

    x2 = np.asarray(L0_attn_data, dtype=np.float32).reshape(NCORES * T, D)
    w = np.asarray(L0_attn_weight, dtype=np.float32)
    mask = np.asarray(L0_attn_mask, dtype=np.int32)
    wg = np.asarray(L0_ffn_w_gate, dtype=np.int32)
    wu = np.asarray(L0_ffn_w_up, dtype=np.int32)
    wd = np.asarray(L0_ffn_w_down, dtype=np.int32)
    sg = np.asarray(L0_ffn_s_gate, dtype=np.float32)
    su = np.asarray(L0_ffn_s_up, dtype=np.float32)
    sd = np.asarray(L0_ffn_s_down, dtype=np.float32)

    hwt = asc(w.reshape(32, 128, 32, 128).transpose(0, 3, 2, 1))
    hwg = asc(wg.reshape(NIT, 128, 32, 128).transpose(0, 3, 2, 1))
    hwu = asc(wu.reshape(NIT, 128, 32, 128).transpose(0, 3, 2, 1))
    hwd = asc(wd.reshape(8, 4, 128, NIT, 128).transpose(3, 4, 0, 1, 2)[P4SEQ])
    sgv = asc(sg.reshape(NIT, 128))
    suv = asc(su.reshape(NIT, 128))
    sdv = asc(sd.reshape(32, 128))

    c = np.arange(D)
    p = np.arange(128)[:, None]
    e32f = (c // 32 == p).astype(ml_dtypes.bfloat16)
    gmat4 = (np.arange(4) == p // 32).astype(np.float32)
    identb = np.eye(128, dtype=ml_dtypes.bfloat16)
    identf = np.eye(128, dtype=np.float32)

    shared = dict(hwt=hwt, mask=mask, hwg=hwg, hwu=hwu, hwd=hwd, sgv=sgv, suv=suv,
                  sdv=sdv, e32f=e32f, gmat4=gmat4, identb=identb, identf=identf)
    in_maps = []
    for ci in range(NCORES):
        xc = asc(x2[ci * T:(ci + 1) * T])
        in_maps.append(dict(x=xc, xT=asc(xc.T), **shared))

    if _NC_CACHE is None:
        _NC_CACHE = build_nc()
    res = run_bass_kernel_spmd(_NC_CACHE, in_maps, core_ids=list(range(NCORES)))
    out = np.concatenate([np.asarray(r["outT"]).T for r in res.results], axis=0)
    return asc(out.reshape(4, 2048, D).astype(np.float32))


# version 17
# speedup vs baseline: 1.0715x; 1.0715x over previous
import numpy as np
import ml_dtypes
from contextlib import ExitStack

import concourse.bass as bass
import concourse.mybir as mybir
from concourse.bass_utils import run_bass_kernel_spmd

F32 = mybir.dt.float32
BF16 = mybir.dt.bfloat16
I32 = mybir.dt.int32
AF = mybir.ActivationFunctionType
ALU = mybir.AluOpType
AX = mybir.AxisListType

NCORES = 8
T = 1024
D = 4096
NI = 11008
NIT = 86
EPS = 1e-5


def _p4_seq():
    seq, ra, sa = [], 0, 48
    for j in range(NIT):
        if (sa >= NIT) or (ra < 48 and (j * 48) // NIT >= ra):
            seq.append(ra); ra += 1
        else:
            seq.append(sa); sa += 1
    return seq


P4SEQ = _p4_seq()


def build_nc():
    nc = bass.Bass("TRN2", target_bir_lowering=False)

    d_x = nc.dram_tensor("x", [T, D], F32, kind="ExternalInput").ap()
    d_xT = nc.dram_tensor("xT", [D, T], F32, kind="ExternalInput").ap()
    d_hwt = nc.dram_tensor("hwt", [32, 128, 32, 128], F32, kind="ExternalInput").ap()
    d_mask = nc.dram_tensor("mask", [D, D], I32, kind="ExternalInput").ap()
    d_hwg = nc.dram_tensor("hwg", [NIT, 128, 32, 128], I32, kind="ExternalInput").ap()
    d_hwu = nc.dram_tensor("hwu", [NIT, 128, 32, 128], I32, kind="ExternalInput").ap()
    d_hwd = nc.dram_tensor("hwd", [NIT, 128, 8, 4, 128], I32, kind="ExternalInput").ap()
    d_sgv = nc.dram_tensor("sgv", [NIT, 128], F32, kind="ExternalInput").ap()
    d_suv = nc.dram_tensor("suv", [NIT, 128], F32, kind="ExternalInput").ap()
    d_sdv = nc.dram_tensor("sdv", [32, 128], F32, kind="ExternalInput").ap()
    d_e32 = nc.dram_tensor("e32f", [128, D], BF16, kind="ExternalInput").ap()
    d_gm4 = nc.dram_tensor("gmat4", [128, 4], F32, kind="ExternalInput").ap()
    d_idb = nc.dram_tensor("identb", [128, 128], BF16, kind="ExternalInput").ap()
    d_idf = nc.dram_tensor("identf", [128, 128], F32, kind="ExternalInput").ap()
    d_out = nc.dram_tensor("outT", [D, T], F32, kind="ExternalOutput").ap()
    d_hTd = nc.dram_tensor("hTd", [NIT, 128, T], BF16, kind="Internal").ap()
    d_sTd = nc.dram_tensor("sTd", [D, T], BF16, kind="Internal").ap()

    with ExitStack() as top:
        sem = lambda n: top.enter_context(nc.semaphore(n))
        sem2 = lambda n: [sem(n + "0"), sem(n + "1")]
        sem4 = lambda n: [sem(n + str(i)) for i in range(4)]
        sm_hT_st = sem2("hT_st")
        sm_sT_st = sem2("sT_st")
        sm_cst = [sem(f"cstp{i}") for i in range(5)]
        sm_mask = sem2("mask")
        sm_x = sem4("xld")
        p1v, p1s, p1t = sem("p1v"), sem("p1s"), sem("p1t")
        sm_wt = sem2("wt")
        sm_cst2 = sem("cst2")
        p2v, p2t = sem("p2v"), sem("p2t")
        sm_wg, sm_wu = sem2("wg"), sem2("wu")
        p3v, p3s, p3t = sem("p3v"), sem("p3s"), sem("p3t")
        sm_hT_ld, sm_wd = sem4("hT_ld"), sem4("wd")
        sm_hr = [sem(f"hr{i}") for i in range(6)]
        sm_xT, sm_sT_ld, sm_out = sem4("xT"), sem4("sT_ld"), sem4("out")
        p4v, p4s, p4t = sem("p4v"), sem("p4s"), sem("p4t")

        sb = lambda n, s, d: top.enter_context(nc.sbuf_tensor(n, s, d))
        sg_cols = sb("sg_cols", [128, NIT], F32)
        su_cols = sb("su_cols", [128, NIT], F32)
        sd_cols = sb("sd_cols", [128, 32], F32)
        wdb = sb("wdb", [128, 16 * 512], BF16)

        with ExitStack() as s123:
            sb3 = lambda n, s, d: s123.enter_context(nc.sbuf_tensor(n, s, d))
            xnT = sb3("xnT", [128, 32 * T], BF16)

            if True:
                with ExitStack() as s01:
                    sb1 = lambda n, s, d: s01.enter_context(nc.sbuf_tensor(n, s, d))
                    ps1 = lambda n, s, d: s01.enter_context(nc.psum_tensor(n, s, d))
                    idb_sb = sb1("idb_sb", [128, 128], BF16)
                    idf_sb = sb1("idf_sb", [128, 128], F32)
                    sgv_sb = sb1("sgv_sb", [NIT, 128], F32)
                    suv_sb = sb1("suv_sb", [NIT, 128], F32)
                    sdv_sb = sb1("sdv_sb", [32, 128], F32)
                    xs = [sb1(f"xs{i}", [128, D], F32) for i in range(4)]
                    xn = [sb1(f"xn{i}", [128, D], BF16) for i in range(4)]
                    st_sum = sb1("st_sum", [128, 8], F32)
                    st_ssq = sb1("st_ssq", [128, 8], F32)
                    st_mean = sb1("st_mean", [128, 8], F32)
                    st_msq = sb1("st_msq", [128, 8], F32)
                    st_m2 = sb1("st_m2", [128, 8], F32)
                    st_var = sb1("st_var", [128, 8], F32)
                    st_sqv = sb1("st_sqv", [128, 8], F32)
                    st_rst = sb1("st_rst", [128, 8], F32)
                    st_nmr = sb1("st_nmr", [128, 8], F32)
                    eps_col = sb1("eps_col", [128, 1], F32)
                    invd_col = sb1("invd_col", [128, 1], F32)
                    ninvd_col = sb1("ninvd_col", [128, 1], F32)
                    p_vec = [ps1(f"p_vec{i}", [128, 512], F32) for i in range(2)]
                    p_tr = [ps1(f"p_tr{i}", [128, 1024], BF16) for i in range(4)]

                    t_vtr = lambda k: k + 1
                    t_trb = lambda j, b: 3 + 4 * j + b + 1
                    s_sq = lambda j: 5 * (j // 2) + (j % 2) + 1
                    s_sqrt_b = lambda b: 5 * b + 3
                    s_xn = lambda j: 5 * (j // 2) + 4 + (j % 2)
                    V = {}

                    with nc.Block("prep") as blk:
                        @blk.vector
                        def _(ve):
                            vc = 0
                            def inc(ins):
                                nonlocal vc
                                ins.then_inc(p1v, 1)
                                vc += 1
                                return vc
                            inc(ve.memset(eps_col[:, :], EPS))
                            inc(ve.memset(invd_col[:, :], 1.0 / D))
                            inc(ve.memset(ninvd_col[:, :], -1.0 / D))
                            for k, dst in enumerate([(sg_cols, NIT), (su_cols, NIT),
                                                     (sd_cols, 32)]):
                                ve.wait_ge(p1t, t_vtr(k))
                                V[f"vecev{k}"] = inc(ve.tensor_copy(
                                    dst[0][:, 0:dst[1]], p_vec[k % 2][:, 0:dst[1]]))
                            def evac_tr(jj):
                                for b in range(4):
                                    ve.wait_ge(p1t, t_trb(jj, b))
                                    dst = xnT[:, :].rearrange(
                                        "p (kc t) -> p kc t", t=T)[
                                        :, b * 8:(b + 1) * 8,
                                        jj * 128:(jj + 1) * 128]
                                    src = p_tr[b][:, :].rearrange(
                                        "p (c t) -> p c t", t=128)
                                    V[f"trev{jj}_{b}"] = inc(ve.tensor_copy(dst, src))
                            def stats(b, sl, sqrt_lm):
                                n_mean = inc(ve.tensor_scalar(
                                    st_mean[:, sl], st_sum[:, sl],
                                    ninvd_col[:, 0:1], None, ALU.mult))
                                inc(ve.tensor_scalar(st_msq[:, sl], st_ssq[:, sl],
                                                     invd_col[:, 0:1], None, ALU.mult))
                                ve.wait_ge(p1v, n_mean)
                                n_m2 = inc(ve.tensor_tensor(
                                    st_m2[:, sl], st_mean[:, sl],
                                    st_mean[:, sl], ALU.mult))
                                ve.wait_ge(p1v, n_m2)
                                V[f"var_{b}"] = inc(ve.tensor_tensor(
                                    st_var[:, sl], st_msq[:, sl], st_m2[:, sl],
                                    ALU.subtract))
                                ve.wait_ge(p1s, sqrt_lm)
                                n_rst = inc(ve.reciprocal(st_rst[:, sl],
                                                          st_sqv[:, sl]))
                                ve.wait_ge(p1v, n_rst)
                                V[f"nmr_{b}"] = inc(ve.tensor_tensor(
                                    st_nmr[:, sl], st_mean[:, sl], st_rst[:, sl],
                                    ALU.mult))
                            for b in range(4):
                                for j in (2 * b, 2 * b + 1):
                                    ve.wait_ge(sm_x[j % 4], 16 * (j // 4 + 1))
                                    V[f"sum{j}"] = inc(ve.reduce_sum(
                                        st_sum[:, j:j + 1], xs[j % 4][:, :], AX.X))
                                ve.wait_ge(p1s, s_sq(2 * b + 1))
                                ve.wait_ge(p1v, V[f"sum{2 * b + 1}"])
                                stats(b, slice(2 * b, 2 * b + 2), s_sqrt_b(b))
                                if b >= 1:
                                    evac_tr(2 * b - 2)
                                    evac_tr(2 * b - 1)
                            evac_tr(6)
                            evac_tr(7)

                        @blk.scalar
                        def _(sc):
                            for i in (1, 3):
                                sc.dma_start(xs[i % 4][:, :],
                                             d_x[i * 128:(i + 1) * 128, :]
                                             ).then_inc(sm_x[i % 4], 16)

                            def half(js, var_lm, nmr_lm):
                                for j in js:
                                    sc.wait_ge(sm_x[j % 4], 16 * (j // 4 + 1))
                                    if j >= 4:
                                        sc.wait_ge(p1s, s_xn(j - 4))
                                        sc.wait_ge(p1t, t_trb(j - 4, 3))
                                    sc.activation(xn[j % 4][:, :], xs[j % 4][:, :],
                                                  AF.Square,
                                                  accum_out=st_ssq[:, j:j + 1]
                                                  ).then_inc(p1s, 1)
                                sc.wait_ge(p1v, V[var_lm])
                                sc.activation(st_sqv[:, js[0]:js[-1] + 1],
                                              st_var[:, js[0]:js[-1] + 1], AF.Sqrt,
                                              bias=eps_col[:, 0:1]).then_inc(p1s, 1)
                                sc.wait_ge(p1v, V[nmr_lm])
                                for j in js:
                                    sc.wait_ge(p1s, s_sq(j))
                                    sc.activation(xn[j % 4][:, :], xs[j % 4][:, :],
                                                  AF.Identity,
                                                  bias=st_nmr[:, j:j + 1],
                                                  scale=st_rst[:, j:j + 1]
                                                  ).then_inc(p1s, 1)
                            for b in range(4):
                                half([2 * b, 2 * b + 1], f"var_{b}", f"nmr_{b}")
                                if b <= 1:
                                    i = 2 * b + 5
                                    sc.wait_ge(p1s, s_xn(i - 4))
                                    sc.dma_start(xs[i % 4][:, :],
                                                 d_x[i * 128:(i + 1) * 128, :]
                                                 ).then_inc(sm_x[i % 4], 16)

                        @blk.tensor
                        def _(te):
                            for i in range(5):
                                te.wait_ge(sm_cst[i], 16)
                            for k, (src, n) in enumerate([(sgv_sb, NIT), (suv_sb, NIT),
                                                          (sdv_sb, 32)]):
                                if k >= 2:
                                    te.wait_ge(p1v, V[f"vecev{k-2}"])
                                np_ = src.shape[0]
                                te.transpose(p_vec[k % 2][:, 0:np_], src[:, :],
                                             idf_sb[0:np_, 0:np_]).then_inc(p1t, 1)
                            for j in range(8):
                                te.wait_ge(p1s, s_xn(j))
                                for b in range(4):
                                    if j >= 1:
                                        te.wait_ge(p1v, V[f"trev{j-1}_{b}"])
                                    for c in range(8):
                                        kc = b * 8 + c
                                        ins = te.transpose(
                                            p_tr[b][:, c * 128:(c + 1) * 128],
                                            xn[j % 4][:, kc * 128:(kc + 1) * 128],
                                            idb_sb[:, :])
                                        if c == 7:
                                            ins.then_inc(p1t, 1)

                        @blk.sync
                        def _(sy):
                            for i in (0, 2):
                                sy.dma_start(xs[i % 4][:, :],
                                             d_x[i * 128:(i + 1) * 128, :]
                                             ).then_inc(sm_x[i % 4], 16)
                            for ci, (dst, src) in enumerate(
                                    [(idb_sb[:, :], d_idb), (idf_sb[:, :], d_idf),
                                     (sgv_sb[:, :], d_sgv), (suv_sb[:, :], d_suv),
                                     (sdv_sb[:, :], d_sdv)]):
                                sy.dma_start(dst, src).then_inc(sm_cst[ci], 16)
                            for i in (4, 6):
                                sy.wait_ge(p1s, s_xn(i - 4))
                                sy.dma_start(xs[i % 4][:, :],
                                             d_x[i * 128:(i + 1) * 128, :]
                                             ).then_inc(sm_x[i % 4], 16)

                with ExitStack() as s2:
                    sb2l = lambda n, s, d: s2.enter_context(nc.sbuf_tensor(n, s, d))
                    ps2 = lambda n, s, d: s2.enter_context(nc.psum_tensor(n, s, d))
                    e32_sb = sb2l("e32_sb", [128, D], BF16)
                    gm4_sb = sb2l("gm4_sb", [128, 4], F32)
                    idb2_sb = sb2l("idb2_sb", [128, 128], BF16)
                    mk = [sb2l(f"mk{i}", [128, D], I32) for i in range(2)]
                    red = [sb2l(f"red{i}", [128, 128], F32) for i in range(2)]
                    bm01t = [sb2l(f"bm01t{i}", [4, 128], BF16) for i in range(2)]
                    bmfeT = sb2l("bmfeT", [128, 128], BF16)
                    wts = [sb2l(f"wts{i}", [128, D], F32) for i in range(2)]
                    wtm = [sb2l(f"wtm{i}", [128, D], BF16) for i in range(2)]
                    sTs = [sb2l(f"sTs{i}", [128, T], BF16) for i in range(2)]
                    p_me4 = ps2("p_me4", [128, 1024], F32)
                    p_sT = [[ps2(f"p_sT{i}_{h}", [128, 512], F32) for h in range(2)]
                            for i in range(2)]
                    p_bm = ps2("p_bm", [128, 512], F32)
                    p_fe = ps2("p_fe", [128, 1024], BF16)

                    NK = 46
                    TL, VL = {}, {}
                    tc = vc = 0
                    for k in range(NK):
                        if 0 <= k - 1 <= 31:
                            tc += 1; TL[("bm4", k - 1)] = tc
                        if 0 <= k - 2 <= 31:
                            tc += 1; TL[("tr4", k - 2)] = tc
                        if k - 3 >= 0 and (k - 3) % 8 == 7 and (k - 3) // 8 <= 3:
                            tc += 32; TL[("exp", (k - 3) // 8)] = tc
                        if 0 <= k - 11 <= 31:
                            tc += 1; TL[("main", k - 11)] = tc
                    for k in range(NK):
                        if k <= 31:
                            vc += 1; VL[("red", k)] = vc
                        if 0 <= k - 1 <= 31:
                            vc += 1; VL[("thresh", k - 1)] = vc
                        if 0 <= k - 2 <= 31:
                            vc += 1; VL[("feev", k - 2)] = vc
                        if 0 <= k - 10 <= 31:
                            vc += 1; VL[("mult", k - 10)] = vc
                        if 0 <= k - 12 <= 31:
                            vc += 2; VL[("evac", k - 12)] = vc

                    with nc.Block("attn") as blk:
                        @blk.sync
                        def _(sy):
                            for ci, (dst, srcd) in enumerate(
                                    [(e32_sb[:, :], d_e32), (gm4_sb[:, :], d_gm4),
                                     (idb2_sb[:, :], d_idb)]):
                                sy.dma_start(dst, srcd).then_inc(sm_cst2, 16)
                                sy.wait_ge(sm_cst2, 16 * (ci + 1))

                            def m_issue(k):
                                if k >= 2:
                                    sy.wait_ge(p2v, VL[("red", k - 2)])
                                sy.dma_start(mk[k % 2][:, :],
                                             d_mask[k * 128:(k + 1) * 128, :]
                                             ).then_inc(sm_mask[k % 2], 16)
                            for k in range(10):
                                m_issue(k)
                            for k in range(32):
                                if k + 10 <= 31:
                                    m_issue(k + 10)
                                if k >= 2:
                                    sy.wait_ge(p2v, VL[("mult", k - 2)])
                                sy.dma_start(
                                    wts[k % 2][:, :].rearrange(
                                        "p (kb n) -> p kb n", n=128),
                                    d_hwt[k]).then_inc(sm_wt[k % 2], 16)

                        @blk.tensor
                        def _(te):
                            def bm4(k):
                                te.wait_ge(p2v, VL[("red", k)])
                                if k >= 1:
                                    te.wait_ge(p2v, VL[("thresh", k - 1)])
                                te.matmul(p_bm[0:4, 0:128], gm4_sb[:, :],
                                          red[k % 2][:, :], start=True,
                                          stop=True).then_inc(p2t, 1)

                            def tr4(k):
                                te.wait_ge(p2v, VL[("thresh", k)])
                                if k >= 1:
                                    te.wait_ge(p2v, VL[("feev", k - 1)])
                                te.transpose(p_fe[:, 0:4], bm01t[k % 2][:, :],
                                             idb2_sb[0:4, 0:4]).then_inc(p2t, 1)

                            def exp(G):
                                te.wait_ge(p2v, VL[("feev", 8 * G + 7)])
                                if G >= 1:
                                    te.wait_ge(p2v, VL[("mult", 8 * G - 1)])
                                for kt in range(32):
                                    ins = te.matmul(
                                        p_me4[:, kt * 32:kt * 32 + 32],
                                        e32_sb[:, kt * 128:(kt + 1) * 128],
                                        bmfeT[:, G * 32:(G + 1) * 32],
                                        start=True, stop=True)
                                ins.then_inc(p2t, 32)

                            def main(nt):
                                par = nt % 2
                                te.wait_ge(p2v, VL[("mult", nt)])
                                if nt >= 2:
                                    te.wait_ge(p2v, VL[("evac", nt - 2)])
                                for kb in range(32):
                                    A = wtm[par][:, kb * 128:(kb + 1) * 128]
                                    for th in range(2):
                                        ins = te.matmul(
                                            p_sT[par][th][:, :], A,
                                            xnT[:, kb * T + th * 512:
                                                kb * T + th * 512 + 512],
                                            start=(kb == 0), stop=(kb == 31))
                                        if kb == 31 and th == 1:
                                            ins.then_inc(p2t, 1)
                            te.wait_ge(sm_cst2, 48)
                            for k in range(NK):
                                if 0 <= k - 1 <= 31:
                                    bm4(k - 1)
                                if 0 <= k - 2 <= 31:
                                    tr4(k - 2)
                                if k - 3 >= 0 and (k - 3) % 8 == 7 and (k - 3) // 8 <= 3:
                                    exp((k - 3) // 8)
                                if 0 <= k - 11 <= 31:
                                    main(k - 11)

                        @blk.vector
                        def _(ve):
                            def redf(k):
                                ve.wait_ge(sm_mask[k % 2], 16 * (k // 2 + 1))
                                if k >= 2:
                                    ve.wait_ge(p2t, TL[("bm4", k - 2)])
                                ve.tensor_reduce(
                                    red[k % 2][:, :],
                                    mk[k % 2][:, :].rearrange("p (s e) -> p s e", e=32),
                                    AX.X, ALU.add).then_inc(p2v, 1)

                            def thresh(k):
                                ve.wait_ge(p2t, TL[("bm4", k)])
                                if k >= 2:
                                    ve.wait_ge(p2t, TL[("tr4", k - 2)])
                                ve.tensor_scalar(bm01t[k % 2][:, :], p_bm[0:4, 0:128],
                                                 0.0, None, ALU.is_gt).then_inc(p2v, 1)

                            def feev(k):
                                ve.wait_ge(p2t, TL[("tr4", k)])
                                ve.tensor_copy(bmfeT[:, 4 * k:4 * k + 4],
                                               p_fe[:, 0:4]).then_inc(p2v, 1)

                            def mult(nt):
                                par = nt % 2
                                G = nt // 8
                                ve.wait_ge(sm_wt[nt % 2], 16 * (nt // 2 + 1))
                                ve.wait_ge(p2t, TL[("exp", G)])
                                if nt >= 2:
                                    ve.wait_ge(p2t, TL[("main", nt - 2)])
                                ap1 = bass.AP(p_me4.ap().tensor,
                                              (nt % 8) * 4,
                                              [[1024, 128], [32, 32], [1, 4], [0, 32]])
                                ve.tensor_tensor(
                                    wtm[par][:, :].rearrange(
                                        "p (a b c) -> p a b c", a=32, b=4),
                                    wts[par][:, :].rearrange(
                                        "p (a b c) -> p a b c", a=32, b=4),
                                    ap1, ALU.mult).then_inc(p2v, 1)

                            def evac(nt):
                                par = nt % 2
                                ve.wait_ge(p2t, TL[("main", nt)])
                                if nt >= 2:
                                    ve.wait_ge(sm_sT_st[nt % 2], 16 * (nt // 2))
                                for th in range(2):
                                    ve.tensor_copy(
                                        sTs[par][:, th * 512:th * 512 + 512],
                                        p_sT[par][th][:, :]).then_inc(p2v, 1)
                            for k in range(NK):
                                if k <= 31:
                                    redf(k)
                                if 0 <= k - 1 <= 31:
                                    thresh(k - 1)
                                if 0 <= k - 2 <= 31:
                                    feev(k - 2)
                                if 0 <= k - 10 <= 31:
                                    mult(k - 10)
                                if 0 <= k - 12 <= 31:
                                    evac(k - 12)

                        @blk.scalar
                        def _(sc):
                            for nt in range(32):
                                sc.wait_ge(p2v, VL[("evac", nt)])
                                sc.dma_start(d_sTd[nt * 128:(nt + 1) * 128, :],
                                             sTs[nt % 2][:, :]
                                             ).then_inc(sm_sT_st[nt % 2], 16)

            with ExitStack() as s3:
                sb3l = lambda n, s, d: s3.enter_context(nc.sbuf_tensor(n, s, d))
                ps3 = lambda n, s, d: s3.enter_context(nc.psum_tensor(n, s, d))
                wgb = [sb3l(f"wgb{i}", [128, D], BF16) for i in range(2)]
                wub = [sb3l(f"wub{i}", [128, D], BF16) for i in range(2)]
                sil = [sb3l(f"sil{i}", [128, T], F32) for i in range(2)]
                uss = [sb3l(f"uss{i}", [128, T], F32) for i in range(2)]
                hTt = [sb3l(f"hTt{i}", [128, T], BF16) for i in range(2)]
                p_g = [[ps3(f"p_g{i}_{h}", [128, 512], F32) for h in range(2)]
                       for i in range(2)]
                p_u = [[ps3(f"p_u{i}_{h}", [128, 512], F32) for h in range(2)]
                       for i in range(2)]

                with nc.Block("ffnA") as blk:
                    @blk.gpsimd
                    def _(gp):
                        for it in range(NIT):
                            if it >= 2:
                                gp.wait_ge(p3t, it - 1)
                            gp.dma_start(
                                wgb[it % 2][:, :].rearrange("p (b i) -> p b i", i=128),
                                d_hwg[it]).then_inc(sm_wg[it % 2], 16)
                            gp.dma_start(
                                wub[it % 2][:, :].rearrange("p (b i) -> p b i", i=128),
                                d_hwu[it]).then_inc(sm_wu[it % 2], 16)
                        for gb in range(2):
                            gp.dma_start(
                                wdb[:, gb * 2048:gb * 2048 + 2048].rearrange(
                                    "p (i hh) -> p i hh", i=4),
                                d_hwd[gb * 4:gb * 4 + 4, :, 0, :, :].rearrange(
                                    "i p a b -> p i (a b)")
                                ).then_inc(sm_wd[gb], 16)

                    @blk.tensor
                    def _(te):
                        for it in range(NIT):
                            par = it % 2
                            te.wait_ge(sm_wg[it % 2], 16 * (it // 2 + 1))
                            te.wait_ge(sm_wu[it % 2], 16 * (it // 2 + 1))
                            if it >= 2:
                                te.wait_ge(p3s, 2 * (it - 2) + 2)
                                te.wait_ge(p3v, 3 * (it - 2) + 2)
                            for w_b, p_x in [(wgb[par], p_g[par]), (wub[par], p_u[par])]:
                                for blk_h in range(32):
                                    A = w_b[:, blk_h * 128:(blk_h + 1) * 128]
                                    for th in range(2):
                                        ins = te.matmul(
                                            p_x[th][:, :], A,
                                            xnT[:, blk_h * T + th * 512:
                                                blk_h * T + th * 512 + 512],
                                            start=(blk_h == 0), stop=(blk_h == 31))
                            ins.then_inc(p3t, 1)

                    @blk.scalar
                    def _(sc):
                        for it in range(NIT):
                            par = it % 2
                            if it >= 1:
                                sc.wait_ge(p3v, 3 * (it - 1) + 3)
                                sc.dma_start(d_hTd[it - 1], hTt[(it - 1) % 2][:, :]
                                             ).then_inc(sm_hT_st[(it - 1) % 2], 16)
                            sc.wait_ge(p3t, it + 1)
                            if it >= 2:
                                sc.wait_ge(p3v, 3 * (it - 2) + 3)
                            for th in range(2):
                                sc.activation(sil[par][:, th * 512:th * 512 + 512],
                                              p_g[par][th][:, :], AF.Silu,
                                              scale=sg_cols[:, it:it + 1]
                                              ).then_inc(p3s, 1)
                        sc.wait_ge(p3v, 3 * (NIT - 1) + 3)
                        sc.dma_start(d_hTd[NIT - 1], hTt[(NIT - 1) % 2][:, :]
                                     ).then_inc(sm_hT_st[(NIT - 1) % 2], 16)

                    @blk.vector
                    def _(ve):
                        for it in range(NIT):
                            par = it % 2
                            ve.wait_ge(p3t, it + 1)
                            if it >= 2:
                                ve.wait_ge(p3v, 3 * it - 3)
                            for th in range(2):
                                ve.tensor_scalar(uss[par][:, th * 512:th * 512 + 512],
                                                 p_u[par][th][:, :],
                                                 su_cols[:, it:it + 1], None,
                                                 ALU.mult).then_inc(p3v, 1)
                            ve.wait_ge(p3s, 2 * it + 2)
                            if it >= 2:
                                ve.wait_ge(sm_hT_st[it % 2], 16 * (it // 2))
                            ve.wait_ge(p3v, 3 * it + 2)
                            ve.tensor_tensor(hTt[par][:, :], sil[par][:, :],
                                             uss[par][:, :], ALU.mult).then_inc(p3v, 1)

        with ExitStack() as s4:
            sb4 = lambda n, s, d: s4.enter_context(nc.sbuf_tensor(n, s, d))
            ps4 = lambda n, s, d: s4.enter_context(nc.psum_tensor(n, s, d))
            hTr = sb4("hTr", [128, 48 * 1024], BF16)
            hTl = [sb4(f"hTl{i}", [128, 2048], BF16) for i in range(4)]
            gTs = [sb4(f"gTs{i}", [128, T], F32) for i in range(4)]
            sTl = [sb4(f"sTl{i}", [128, T], F32) for i in range(4)]
            xTl = [sb4(f"xTl{i}", [128, T], F32) for i in range(4)]
            outs = [sb4(f"outs{i}", [128, T], F32) for i in range(4)]
            tmpm = sb4("tmpm", [128, T], F32)
            p_acc = [[ps4(f"p_acc{h}_{t}", [128, 512], F32) for t in range(2)]
                     for h in range(4)]

            NB = 22
            NPAIR = 19
            SEQ = P4SEQ
            assert sorted(SEQ) == list(range(NIT))
            POS = {it: j for j, it in enumerate(SEQ)}
            ht_use = {}
            _cnt4 = [0, 0, 0, 0]
            for _g in range(8):
                for _p in range(NPAIR):
                    _cnt4[_p % 4] += 1
                    ht_use[(_g, _p)] = _cnt4[_p % 4]
            wd_use = {}
            _cnt4w = [0, 0, 0, 0]
            for _g in range(8):
                for _b in range(NB):
                    _gb = _g * NB + _b
                    _cnt4w[_gb % 4] += 1
                    wd_use[(_g, _b)] = _cnt4w[_gb % 4]

            with nc.Block("down") as blk:
                @blk.sync
                def _(sy):
                    for r in range(6):
                        sy.wait_ge(sm_hT_st[0], 16 * (4 * r + 4))
                        sy.wait_ge(sm_hT_st[1], 16 * (4 * r + 4))
                        sy.dma_start(
                            hTr[:, r * 8192:(r + 1) * 8192].rearrange(
                                "q (i t) -> q i t", i=8),
                            d_hTd[8 * r:8 * r + 8, :, :].rearrange(
                                "i q t -> q i t")).then_inc(sm_hr[r], 16)
                    sy.wait_ge(sm_hT_st[0], 16 * 43)
                    sy.wait_ge(sm_hT_st[1], 16 * 43)
                    sy.wait_ge(sm_sT_st[0], 16 * 16)
                    sy.wait_ge(sm_sT_st[1], 16 * 16)
                    for g in range(8):
                        for p in range(NPAIR):
                            if p >= 4:
                                sy.wait_ge(p4t,
                                           g * NIT + POS[48 + 2 * (p - 4) + 1] + 1)
                            elif g >= 1:
                                q = {0: 16, 1: 17, 2: 18, 3: 15}[p]
                                sy.wait_ge(p4t,
                                           (g - 1) * NIT + POS[48 + 2 * q + 1] + 1)
                            sy.dma_start(
                                hTl[p % 4][:, :].rearrange(
                                    "q (i t) -> q i t", i=2),
                                d_hTd[48 + 2 * p:48 + 2 * p + 2, :, :].rearrange(
                                    "i q t -> q i t")).then_inc(sm_hT_ld[p % 4], 16)
                            if p == 2:
                                for ht in range(4):
                                    hg = g * 4 + ht
                                    if g >= 1:
                                        sy.wait_ge(p4v, 8 * (g - 1) + 2 * (ht + 1))
                                    sy.dma_start(xTl[ht][:, :],
                                                 d_xT[hg * 128:(hg + 1) * 128, :]
                                                 ).then_inc(sm_xT[ht], 16)
                    for ht in range(4):
                        sy.wait_ge(sm_out[ht], 16 * 8)

                @blk.gpsimd
                def _(gp):
                    for g in range(8):
                        for b in range(NB):
                            it0 = b * 4
                            nit = min(4, NIT - it0)
                            gb = g * NB + b
                            if gb < 2:
                                continue
                            if gb >= 4:
                                pgb = gb - 4
                                pg, pb = pgb // NB, pgb % NB
                                gp.wait_ge(p4t,
                                           pg * NIT + min(4 * (pb + 1), NIT))
                            sl = (gb % 4) * 2048
                            dst = wdb[:, sl:sl + nit * 512]
                            gp.dma_start(
                                dst.rearrange("p (i hh) -> p i hh", i=nit),
                                d_hwd[it0:it0 + nit, :, g, :, :].rearrange(
                                    "i p a b -> p i (a b)")
                                ).then_inc(sm_wd[gb % 4], 16)
                        for ht in range(4):
                            hg = g * 4 + ht
                            if g >= 1:
                                gp.wait_ge(p4v, 8 * (g - 1) + 2 * (ht + 1))
                            gp.wait_ge(sm_sT_st[hg % 2], 16 * (hg // 2 + 1))
                            gp.dma_start(sTl[ht][:, :],
                                         d_sTd[hg * 128:(hg + 1) * 128, :]
                                         ).then_inc(sm_sT_ld[ht], 16)

                @blk.tensor
                def _(te):
                    for g in range(8):
                        for j in range(NIT):
                            it = SEQ[j]
                            b = j // 4
                            gb = g * NB + b
                            te.wait_ge(sm_wd[gb % 4], 16 * wd_use[(g, b)])
                            if it < 48:
                                if g == 0:
                                    te.wait_ge(sm_hr[it // 8], 16)
                                slot, sbase = hTr, it * 1024
                            else:
                                pr = (it - 48) // 2
                                te.wait_ge(sm_hT_ld[pr % 4],
                                           16 * ht_use[(g, pr)])
                                slot, sbase = hTl[pr % 4], (it % 2) * 1024
                            base = (gb % 4) * 2048 + (j % 4) * 512
                            for ht in range(4):
                                A = wdb[:, base + ht * 128:base + ht * 128 + 128]
                                for th in range(2):
                                    if j == 0 and g >= 1:
                                        te.wait_ge(p4s,
                                                   8 * (g - 1) + 2 * ht + th + 1)
                                    B = slot[:, sbase + th * 512:
                                             sbase + th * 512 + 512]
                                    ins = te.matmul(p_acc[ht][th][:, :], A, B,
                                                    start=(j == 0), stop=(j == NIT - 1))
                            ins.then_inc(p4t, 1)

                @blk.scalar
                def _(sc):
                    for g in range(8):
                        sc.wait_ge(p4t, g * NIT + NIT)
                        for ht in range(4):
                            if g >= 1:
                                sc.wait_ge(p4v, 8 * (g - 1) + 2 * (ht + 1))
                            for th in range(2):
                                sc.activation(gTs[ht][:, th * 512:th * 512 + 512],
                                              p_acc[ht][th][:, :], AF.Silu,
                                              scale=sd_cols[:, g * 4 + ht:g * 4 + ht + 1]
                                              ).then_inc(p4s, 1)
                        for ht in range(4):
                            hg = g * 4 + ht
                            sc.wait_ge(p4v, 8 * g + 2 * (ht + 1))
                            sc.dma_start(d_out[hg * 128:(hg + 1) * 128, :],
                                         outs[ht][:, :]).then_inc(sm_out[ht], 16)

                @blk.vector
                def _(ve):
                    for g in range(8):
                        for ht in range(4):
                            k = 4 * g + ht
                            ve.wait_ge(p4s, 8 * g + 2 * (ht + 1))
                            ve.wait_ge(sm_xT[ht], 16 * (g + 1))
                            ve.wait_ge(sm_sT_ld[ht], 16 * (g + 1))
                            if g >= 1:
                                ve.wait_ge(sm_out[ht], 16 * g)
                            if k >= 1:
                                ve.wait_ge(p4v, 2 * k)
                            ve.tensor_tensor(tmpm[:, :], gTs[ht][:, :], sTl[ht][:, :],
                                             ALU.mult).then_inc(p4v, 1)
                            ve.wait_ge(p4v, 2 * k + 1)
                            ve.tensor_tensor(outs[ht][:, :], tmpm[:, :], xTl[ht][:, :],
                                             ALU.add).then_inc(p4v, 1)

    nc.finalize()
    return nc


_NC_CACHE = None


def kernel(L0_attn_data, L0_attn_weight, L0_attn_mask, L0_ffn_w_gate, L0_ffn_s_gate,
           L0_ffn_w_up, L0_ffn_s_up, L0_ffn_w_down, L0_ffn_s_down):
    global _NC_CACHE
    asc = np.ascontiguousarray

    x2 = np.asarray(L0_attn_data, dtype=np.float32).reshape(NCORES * T, D)
    w = np.asarray(L0_attn_weight, dtype=np.float32)
    mask = np.asarray(L0_attn_mask, dtype=np.int32)
    wg = np.asarray(L0_ffn_w_gate, dtype=np.int32)
    wu = np.asarray(L0_ffn_w_up, dtype=np.int32)
    wd = np.asarray(L0_ffn_w_down, dtype=np.int32)
    sg = np.asarray(L0_ffn_s_gate, dtype=np.float32)
    su = np.asarray(L0_ffn_s_up, dtype=np.float32)
    sd = np.asarray(L0_ffn_s_down, dtype=np.float32)

    hwt = asc(w.reshape(32, 128, 32, 128).transpose(0, 3, 2, 1))
    hwg = asc(wg.reshape(NIT, 128, 32, 128).transpose(0, 3, 2, 1))
    hwu = asc(wu.reshape(NIT, 128, 32, 128).transpose(0, 3, 2, 1))
    hwd = asc(wd.reshape(8, 4, 128, NIT, 128).transpose(3, 4, 0, 1, 2)[P4SEQ])
    sgv = asc(sg.reshape(NIT, 128))
    suv = asc(su.reshape(NIT, 128))
    sdv = asc(sd.reshape(32, 128))

    c = np.arange(D)
    p = np.arange(128)[:, None]
    e32f = (c // 32 == p).astype(ml_dtypes.bfloat16)
    gmat4 = (np.arange(4) == p // 32).astype(np.float32)
    identb = np.eye(128, dtype=ml_dtypes.bfloat16)
    identf = np.eye(128, dtype=np.float32)

    shared = dict(hwt=hwt, mask=mask, hwg=hwg, hwu=hwu, hwd=hwd, sgv=sgv, suv=suv,
                  sdv=sdv, e32f=e32f, gmat4=gmat4, identb=identb, identf=identf)
    in_maps = []
    for ci in range(NCORES):
        xc = asc(x2[ci * T:(ci + 1) * T])
        in_maps.append(dict(x=xc, xT=asc(xc.T), **shared))

    if _NC_CACHE is None:
        _NC_CACHE = build_nc()
    res = run_bass_kernel_spmd(_NC_CACHE, in_maps, core_ids=list(range(NCORES)))
    out = np.concatenate([np.asarray(r["outT"]).T for r in res.results], axis=0)
    return asc(out.reshape(4, 2048, D).astype(np.float32))


# version 18
# speedup vs baseline: 1.1444x; 1.0680x over previous
import numpy as np
import ml_dtypes
from contextlib import ExitStack

import concourse.bass as bass
import concourse.mybir as mybir
from concourse.bass_utils import run_bass_kernel_spmd

F32 = mybir.dt.float32
BF16 = mybir.dt.bfloat16
I32 = mybir.dt.int32
AF = mybir.ActivationFunctionType
ALU = mybir.AluOpType
AX = mybir.AxisListType

NCORES = 8
T = 1024
D = 4096
NI = 11008
NIT = 86
EPS = 1e-5


def _p4_seq():
    seq, ra, sa = [], 0, 48
    for j in range(NIT):
        if (sa >= NIT) or (ra < 48 and (j * 48) // NIT >= ra):
            seq.append(ra); ra += 1
        else:
            seq.append(sa); sa += 1
    return seq


P4SEQ = _p4_seq()


def build_nc():
    nc = bass.Bass("TRN2", target_bir_lowering=False)

    d_x = nc.dram_tensor("x", [T, D], F32, kind="ExternalInput").ap()
    d_xT = nc.dram_tensor("xT", [D, T], F32, kind="ExternalInput").ap()
    d_hwt = nc.dram_tensor("hwt", [32, 128, 32, 128], F32, kind="ExternalInput").ap()
    d_mask = nc.dram_tensor("mask", [D, D], I32, kind="ExternalInput").ap()
    d_hwg = nc.dram_tensor("hwg", [NIT, 128, 32, 128], I32, kind="ExternalInput").ap()
    d_hwu = nc.dram_tensor("hwu", [NIT, 128, 32, 128], I32, kind="ExternalInput").ap()
    d_hwd = nc.dram_tensor("hwd", [NIT, 128, 8, 4, 128], I32, kind="ExternalInput").ap()
    d_sgv = nc.dram_tensor("sgv", [NIT, 128], F32, kind="ExternalInput").ap()
    d_suv = nc.dram_tensor("suv", [NIT, 128], F32, kind="ExternalInput").ap()
    d_sdv = nc.dram_tensor("sdv", [32, 128], F32, kind="ExternalInput").ap()
    d_e32 = nc.dram_tensor("e32f", [128, D], BF16, kind="ExternalInput").ap()
    d_gm4 = nc.dram_tensor("gmat4", [128, 4], F32, kind="ExternalInput").ap()
    d_idb = nc.dram_tensor("identb", [128, 128], BF16, kind="ExternalInput").ap()
    d_idf = nc.dram_tensor("identf", [128, 128], F32, kind="ExternalInput").ap()
    d_out = nc.dram_tensor("outT", [D, T], F32, kind="ExternalOutput").ap()
    d_hTd = nc.dram_tensor("hTd", [NIT, 128, T], BF16, kind="Internal").ap()
    d_sTd = nc.dram_tensor("sTd", [D, T], BF16, kind="Internal").ap()

    with ExitStack() as top:
        sem = lambda n: top.enter_context(nc.semaphore(n))
        sem2 = lambda n: [sem(n + "0"), sem(n + "1")]
        sem4 = lambda n: [sem(n + str(i)) for i in range(4)]
        sm_hT_st = sem2("hT_st")
        sm_sT_st = sem2("sT_st")
        sm_cst = [sem(f"cstp{i}") for i in range(5)]
        sm_mask = sem2("mask")
        sm_x = sem4("xld")
        p1v, p1s, p1t = sem("p1v"), sem("p1s"), sem("p1t")
        sm_wt = sem2("wt")
        sm_cst2 = sem("cst2")
        p2v, p2t = sem("p2v"), sem("p2t")
        sm_wg, sm_wu = sem2("wg"), sem2("wu")
        p3v, p3s, p3t = sem("p3v"), sem("p3s"), sem("p3t")
        sm_hT_ld, sm_wd = sem4("hT_ld"), sem4("wd")
        sm_hr = [sem(f"hr{i}") for i in range(6)]
        sm_xT, sm_sT_ld, sm_out = sem4("xT"), sem4("sT_ld"), sem4("out")
        p4v, p4s, p4t = sem("p4v"), sem("p4s"), sem("p4t")

        sb = lambda n, s, d: top.enter_context(nc.sbuf_tensor(n, s, d))
        sg_cols = sb("sg_cols", [128, NIT], F32)
        su_cols = sb("su_cols", [128, NIT], F32)
        sd_cols = sb("sd_cols", [128, 32], F32)
        wdb = sb("wdb", [128, 16 * 512], BF16)

        with ExitStack() as s123:
            sb3 = lambda n, s, d: s123.enter_context(nc.sbuf_tensor(n, s, d))
            xnT = sb3("xnT", [128, 32 * T], BF16)

            if True:
                with ExitStack() as s01:
                    sb1 = lambda n, s, d: s01.enter_context(nc.sbuf_tensor(n, s, d))
                    ps1 = lambda n, s, d: s01.enter_context(nc.psum_tensor(n, s, d))
                    idb_sb = sb1("idb_sb", [128, 128], BF16)
                    idf_sb = sb1("idf_sb", [128, 128], F32)
                    sgv_sb = sb1("sgv_sb", [NIT, 128], F32)
                    suv_sb = sb1("suv_sb", [NIT, 128], F32)
                    sdv_sb = sb1("sdv_sb", [32, 128], F32)
                    xs = [sb1(f"xs{i}", [128, D], F32) for i in range(4)]
                    xn = [sb1(f"xn{i}", [128, D], BF16) for i in range(4)]
                    st_sum = sb1("st_sum", [128, 8], F32)
                    st_ssq = sb1("st_ssq", [128, 8], F32)
                    st_mean = sb1("st_mean", [128, 8], F32)
                    st_msq = sb1("st_msq", [128, 8], F32)
                    st_m2 = sb1("st_m2", [128, 8], F32)
                    st_var = sb1("st_var", [128, 8], F32)
                    st_sqv = sb1("st_sqv", [128, 8], F32)
                    st_rst = sb1("st_rst", [128, 8], F32)
                    st_nmr = sb1("st_nmr", [128, 8], F32)
                    eps_col = sb1("eps_col", [128, 1], F32)
                    invd_col = sb1("invd_col", [128, 1], F32)
                    ninvd_col = sb1("ninvd_col", [128, 1], F32)
                    p_vec = [ps1(f"p_vec{i}", [128, 512], F32) for i in range(2)]
                    p_tr = [ps1(f"p_tr{i}", [128, 1024], BF16) for i in range(4)]

                    t_vtr = lambda k: k + 1
                    t_trb = lambda j, b: 3 + 4 * j + b + 1
                    s_sq = lambda j: j + 1 if j < 4 else j + 6
                    S_SQRT_A, S_SQRT_B = 5, 14
                    s_xn = lambda j: j + 6 if j < 4 else j + 11
                    V = {}

                    with nc.Block("prep") as blk:
                        @blk.vector
                        def _(ve):
                            vc = 0
                            def inc(ins):
                                nonlocal vc
                                ins.then_inc(p1v, 1)
                                vc += 1
                                return vc
                            inc(ve.memset(eps_col[:, :], EPS))
                            inc(ve.memset(invd_col[:, :], 1.0 / D))
                            inc(ve.memset(ninvd_col[:, :], -1.0 / D))
                            for k, dst in enumerate([(sg_cols, NIT), (su_cols, NIT),
                                                     (sd_cols, 32)]):
                                ve.wait_ge(p1t, t_vtr(k))
                                V[f"vecev{k}"] = inc(ve.tensor_copy(
                                    dst[0][:, 0:dst[1]], p_vec[k % 2][:, 0:dst[1]]))
                            def evac_tr(jj):
                                for b in range(4):
                                    ve.wait_ge(p1t, t_trb(jj, b))
                                    dst = xnT[:, :].rearrange(
                                        "p (kc t) -> p kc t", t=T)[
                                        :, b * 8:(b + 1) * 8,
                                        jj * 128:(jj + 1) * 128]
                                    src = p_tr[b][:, :].rearrange(
                                        "p (c t) -> p c t", t=128)
                                    V[f"trev{jj}_{b}"] = inc(ve.tensor_copy(dst, src))
                            def stats(h, sl, sqrt_lm):
                                n_mean = inc(ve.tensor_scalar(
                                    st_mean[:, sl], st_sum[:, sl],
                                    ninvd_col[:, 0:1], None, ALU.mult))
                                inc(ve.tensor_scalar(st_msq[:, sl], st_ssq[:, sl],
                                                     invd_col[:, 0:1], None, ALU.mult))
                                ve.wait_ge(p1v, n_mean)
                                n_m2 = inc(ve.tensor_tensor(
                                    st_m2[:, sl], st_mean[:, sl],
                                    st_mean[:, sl], ALU.mult))
                                ve.wait_ge(p1v, n_m2)
                                V[f"var_{h}"] = inc(ve.tensor_tensor(
                                    st_var[:, sl], st_msq[:, sl], st_m2[:, sl],
                                    ALU.subtract))
                                ve.wait_ge(p1s, sqrt_lm)
                                n_rst = inc(ve.reciprocal(st_rst[:, sl],
                                                          st_sqv[:, sl]))
                                ve.wait_ge(p1v, n_rst)
                                V[f"nmr_{h}"] = inc(ve.tensor_tensor(
                                    st_nmr[:, sl], st_mean[:, sl], st_rst[:, sl],
                                    ALU.mult))
                            for j in range(4):
                                ve.wait_ge(sm_x[j % 4], 16)
                                V[f"sum{j}"] = inc(ve.reduce_sum(
                                    st_sum[:, j:j + 1], xs[j % 4][:, :], AX.X))
                            ve.wait_ge(p1s, s_sq(3))
                            ve.wait_ge(p1v, V["sum3"])
                            stats("a", slice(0, 4), S_SQRT_A)
                            for j in range(4, 8):
                                ve.wait_ge(sm_x[j % 4], 32)
                                V[f"sum{j}"] = inc(ve.reduce_sum(
                                    st_sum[:, j:j + 1], xs[j % 4][:, :], AX.X))
                                if j >= 5:
                                    evac_tr(j - 5)
                            ve.wait_ge(p1s, s_sq(7))
                            ve.wait_ge(p1v, V["sum7"])
                            stats("b", slice(4, 8), S_SQRT_B)
                            for j in range(3, 8):
                                evac_tr(j)

                        @blk.scalar
                        def _(sc):
                            for i in (1, 3):
                                sc.dma_start(xs[i % 4][:, :],
                                             d_x[i * 128:(i + 1) * 128, :]
                                             ).then_inc(sm_x[i % 4], 16)

                            def half(js, var_lm, nmr_lm):
                                for j in js:
                                    sc.wait_ge(sm_x[j % 4], 16 * (j // 4 + 1))
                                    if j >= 4:
                                        sc.wait_ge(p1s, s_xn(j - 4))
                                        sc.wait_ge(p1t, t_trb(j - 4, 3))
                                    sc.activation(xn[j % 4][:, :], xs[j % 4][:, :],
                                                  AF.Square,
                                                  accum_out=st_ssq[:, j:j + 1]
                                                  ).then_inc(p1s, 1)
                                sc.wait_ge(p1v, V[var_lm])
                                sc.activation(st_sqv[:, js[0]:js[-1] + 1],
                                              st_var[:, js[0]:js[-1] + 1], AF.Sqrt,
                                              bias=eps_col[:, 0:1]).then_inc(p1s, 1)
                                sc.wait_ge(p1v, V[nmr_lm])
                                for j in js:
                                    sc.wait_ge(p1s, s_sq(j))
                                    sc.activation(xn[j % 4][:, :], xs[j % 4][:, :],
                                                  AF.Identity,
                                                  bias=st_nmr[:, j:j + 1],
                                                  scale=st_rst[:, j:j + 1]
                                                  ).then_inc(p1s, 1)
                            half([0, 1, 2, 3], "var_a", "nmr_a")
                            for i in (5, 7):
                                sc.wait_ge(p1s, s_xn(i - 4))
                                sc.dma_start(xs[i % 4][:, :],
                                             d_x[i * 128:(i + 1) * 128, :]
                                             ).then_inc(sm_x[i % 4], 16)
                            half([4, 5, 6, 7], "var_b", "nmr_b")

                        @blk.tensor
                        def _(te):
                            for i in range(5):
                                te.wait_ge(sm_cst[i], 16)
                            for k, (src, n) in enumerate([(sgv_sb, NIT), (suv_sb, NIT),
                                                          (sdv_sb, 32)]):
                                if k >= 2:
                                    te.wait_ge(p1v, V[f"vecev{k-2}"])
                                np_ = src.shape[0]
                                te.transpose(p_vec[k % 2][:, 0:np_], src[:, :],
                                             idf_sb[0:np_, 0:np_]).then_inc(p1t, 1)
                            for j in range(8):
                                te.wait_ge(p1s, s_xn(j))
                                for b in range(4):
                                    if j >= 1:
                                        te.wait_ge(p1v, V[f"trev{j-1}_{b}"])
                                    for c in range(8):
                                        kc = b * 8 + c
                                        ins = te.transpose(
                                            p_tr[b][:, c * 128:(c + 1) * 128],
                                            xn[j % 4][:, kc * 128:(kc + 1) * 128],
                                            idb_sb[:, :])
                                        if c == 7:
                                            ins.then_inc(p1t, 1)

                        @blk.sync
                        def _(sy):
                            for i in (0, 2):
                                sy.dma_start(xs[i % 4][:, :],
                                             d_x[i * 128:(i + 1) * 128, :]
                                             ).then_inc(sm_x[i % 4], 16)
                            for ci, (dst, src) in enumerate(
                                    [(idb_sb[:, :], d_idb), (idf_sb[:, :], d_idf),
                                     (sgv_sb[:, :], d_sgv), (suv_sb[:, :], d_suv),
                                     (sdv_sb[:, :], d_sdv)]):
                                sy.dma_start(dst, src).then_inc(sm_cst[ci], 16)
                            for i in (4, 6):
                                sy.wait_ge(p1s, s_xn(i - 4))
                                sy.dma_start(xs[i % 4][:, :],
                                             d_x[i * 128:(i + 1) * 128, :]
                                             ).then_inc(sm_x[i % 4], 16)

                with ExitStack() as s2:
                    sb2l = lambda n, s, d: s2.enter_context(nc.sbuf_tensor(n, s, d))
                    ps2 = lambda n, s, d: s2.enter_context(nc.psum_tensor(n, s, d))
                    e32_sb = sb2l("e32_sb", [128, D], BF16)
                    gm4_sb = sb2l("gm4_sb", [128, 4], F32)
                    idb2_sb = sb2l("idb2_sb", [128, 128], BF16)
                    mk = [sb2l(f"mk{i}", [128, D], I32) for i in range(2)]
                    red = [sb2l(f"red{i}", [128, 128], F32) for i in range(2)]
                    bm01t = [sb2l(f"bm01t{i}", [4, 128], BF16) for i in range(2)]
                    bmfeT = sb2l("bmfeT", [128, 128], BF16)
                    wts = [sb2l(f"wts{i}", [128, D], F32) for i in range(2)]
                    wtm = [sb2l(f"wtm{i}", [128, D], BF16) for i in range(2)]
                    sTs = [sb2l(f"sTs{i}", [128, T], BF16) for i in range(2)]
                    p_me4 = ps2("p_me4", [128, 1024], F32)
                    p_sT = [[ps2(f"p_sT{i}_{h}", [128, 512], F32) for h in range(2)]
                            for i in range(2)]
                    p_bm = ps2("p_bm", [128, 512], F32)
                    p_fe = ps2("p_fe", [128, 1024], BF16)

                    NK = 46
                    TL, VL = {}, {}
                    tc = vc = 0
                    for k in range(NK):
                        if 0 <= k - 1 <= 31:
                            tc += 1; TL[("bm4", k - 1)] = tc
                        if 0 <= k - 2 <= 31:
                            tc += 1; TL[("tr4", k - 2)] = tc
                        if k - 3 >= 0 and (k - 3) % 8 == 7 and (k - 3) // 8 <= 3:
                            tc += 32; TL[("exp", (k - 3) // 8)] = tc
                        if 0 <= k - 11 <= 31:
                            tc += 1; TL[("main", k - 11)] = tc
                    for k in range(NK):
                        if k <= 31:
                            vc += 1; VL[("red", k)] = vc
                        if 0 <= k - 1 <= 31:
                            vc += 1; VL[("thresh", k - 1)] = vc
                        if 0 <= k - 2 <= 31:
                            vc += 1; VL[("feev", k - 2)] = vc
                        if 0 <= k - 10 <= 31:
                            vc += 1; VL[("mult", k - 10)] = vc
                        if 0 <= k - 12 <= 31:
                            vc += 2; VL[("evac", k - 12)] = vc

                    with nc.Block("attn") as blk:
                        @blk.sync
                        def _(sy):
                            for ci, (dst, srcd) in enumerate(
                                    [(e32_sb[:, :], d_e32), (gm4_sb[:, :], d_gm4),
                                     (idb2_sb[:, :], d_idb)]):
                                sy.dma_start(dst, srcd).then_inc(sm_cst2, 16)
                                sy.wait_ge(sm_cst2, 16 * (ci + 1))

                            def m_issue(k):
                                if k >= 2:
                                    sy.wait_ge(p2v, VL[("red", k - 2)])
                                sy.dma_start(mk[k % 2][:, :],
                                             d_mask[k * 128:(k + 1) * 128, :]
                                             ).then_inc(sm_mask[k % 2], 16)
                            for k in range(10):
                                m_issue(k)
                            for k in range(32):
                                if k + 10 <= 31:
                                    m_issue(k + 10)
                                if k >= 2:
                                    sy.wait_ge(p2v, VL[("mult", k - 2)])
                                sy.dma_start(
                                    wts[k % 2][:, :].rearrange(
                                        "p (kb n) -> p kb n", n=128),
                                    d_hwt[k]).then_inc(sm_wt[k % 2], 16)

                        @blk.tensor
                        def _(te):
                            def bm4(k):
                                te.wait_ge(p2v, VL[("red", k)])
                                if k >= 1:
                                    te.wait_ge(p2v, VL[("thresh", k - 1)])
                                te.matmul(p_bm[0:4, 0:128], gm4_sb[:, :],
                                          red[k % 2][:, :], start=True,
                                          stop=True).then_inc(p2t, 1)

                            def tr4(k):
                                te.wait_ge(p2v, VL[("thresh", k)])
                                if k >= 1:
                                    te.wait_ge(p2v, VL[("feev", k - 1)])
                                te.transpose(p_fe[:, 0:4], bm01t[k % 2][:, :],
                                             idb2_sb[0:4, 0:4]).then_inc(p2t, 1)

                            def exp(G):
                                te.wait_ge(p2v, VL[("feev", 8 * G + 7)])
                                if G >= 1:
                                    te.wait_ge(p2v, VL[("mult", 8 * G - 1)])
                                for kt in range(32):
                                    ins = te.matmul(
                                        p_me4[:, kt * 32:kt * 32 + 32],
                                        e32_sb[:, kt * 128:(kt + 1) * 128],
                                        bmfeT[:, G * 32:(G + 1) * 32],
                                        start=True, stop=True)
                                ins.then_inc(p2t, 32)

                            def main(nt):
                                par = nt % 2
                                te.wait_ge(p2v, VL[("mult", nt)])
                                if nt >= 2:
                                    te.wait_ge(p2v, VL[("evac", nt - 2)])
                                for kb in range(32):
                                    A = wtm[par][:, kb * 128:(kb + 1) * 128]
                                    for th in range(2):
                                        ins = te.matmul(
                                            p_sT[par][th][:, :], A,
                                            xnT[:, kb * T + th * 512:
                                                kb * T + th * 512 + 512],
                                            start=(kb == 0), stop=(kb == 31))
                                        if kb == 31 and th == 1:
                                            ins.then_inc(p2t, 1)
                            te.wait_ge(sm_cst2, 48)
                            for k in range(NK):
                                if 0 <= k - 1 <= 31:
                                    bm4(k - 1)
                                if 0 <= k - 2 <= 31:
                                    tr4(k - 2)
                                if k - 3 >= 0 and (k - 3) % 8 == 7 and (k - 3) // 8 <= 3:
                                    exp((k - 3) // 8)
                                if 0 <= k - 11 <= 31:
                                    main(k - 11)

                        @blk.vector
                        def _(ve):
                            def redf(k):
                                ve.wait_ge(sm_mask[k % 2], 16 * (k // 2 + 1))
                                if k >= 2:
                                    ve.wait_ge(p2t, TL[("bm4", k - 2)])
                                ve.tensor_reduce(
                                    red[k % 2][:, :],
                                    mk[k % 2][:, :].rearrange("p (s e) -> p s e", e=32),
                                    AX.X, ALU.add).then_inc(p2v, 1)

                            def thresh(k):
                                ve.wait_ge(p2t, TL[("bm4", k)])
                                if k >= 2:
                                    ve.wait_ge(p2t, TL[("tr4", k - 2)])
                                ve.tensor_scalar(bm01t[k % 2][:, :], p_bm[0:4, 0:128],
                                                 0.0, None, ALU.is_gt).then_inc(p2v, 1)

                            def feev(k):
                                ve.wait_ge(p2t, TL[("tr4", k)])
                                ve.tensor_copy(bmfeT[:, 4 * k:4 * k + 4],
                                               p_fe[:, 0:4]).then_inc(p2v, 1)

                            def mult(nt):
                                par = nt % 2
                                G = nt // 8
                                ve.wait_ge(sm_wt[nt % 2], 16 * (nt // 2 + 1))
                                ve.wait_ge(p2t, TL[("exp", G)])
                                if nt >= 2:
                                    ve.wait_ge(p2t, TL[("main", nt - 2)])
                                ap1 = bass.AP(p_me4.ap().tensor,
                                              (nt % 8) * 4,
                                              [[1024, 128], [32, 32], [1, 4], [0, 32]])
                                ve.tensor_tensor(
                                    wtm[par][:, :].rearrange(
                                        "p (a b c) -> p a b c", a=32, b=4),
                                    wts[par][:, :].rearrange(
                                        "p (a b c) -> p a b c", a=32, b=4),
                                    ap1, ALU.mult).then_inc(p2v, 1)

                            def evac(nt):
                                par = nt % 2
                                ve.wait_ge(p2t, TL[("main", nt)])
                                if nt >= 2:
                                    ve.wait_ge(sm_sT_st[nt % 2], 16 * (nt // 2))
                                for th in range(2):
                                    ve.tensor_copy(
                                        sTs[par][:, th * 512:th * 512 + 512],
                                        p_sT[par][th][:, :]).then_inc(p2v, 1)
                            for k in range(NK):
                                if k <= 31:
                                    redf(k)
                                if 0 <= k - 1 <= 31:
                                    thresh(k - 1)
                                if 0 <= k - 2 <= 31:
                                    feev(k - 2)
                                if 0 <= k - 10 <= 31:
                                    mult(k - 10)
                                if 0 <= k - 12 <= 31:
                                    evac(k - 12)

                        @blk.scalar
                        def _(sc):
                            for nt in range(32):
                                sc.wait_ge(p2v, VL[("evac", nt)])
                                sc.dma_start(d_sTd[nt * 128:(nt + 1) * 128, :],
                                             sTs[nt % 2][:, :]
                                             ).then_inc(sm_sT_st[nt % 2], 16)

            with ExitStack() as s3:
                sb3l = lambda n, s, d: s3.enter_context(nc.sbuf_tensor(n, s, d))
                ps3 = lambda n, s, d: s3.enter_context(nc.psum_tensor(n, s, d))
                wgb = [sb3l(f"wgb{i}", [128, D], BF16) for i in range(2)]
                wub = [sb3l(f"wub{i}", [128, D], BF16) for i in range(2)]
                sil = [sb3l(f"sil{i}", [128, T], F32) for i in range(2)]
                uss = [sb3l(f"uss{i}", [128, T], F32) for i in range(2)]
                hTt = [sb3l(f"hTt{i}", [128, T], BF16) for i in range(2)]
                p_g = [[ps3(f"p_g{i}_{h}", [128, 512], F32) for h in range(2)]
                       for i in range(2)]
                p_u = [[ps3(f"p_u{i}_{h}", [128, 512], F32) for h in range(2)]
                       for i in range(2)]

                with nc.Block("ffnA") as blk:
                    @blk.gpsimd
                    def _(gp):
                        for it in range(NIT):
                            if it >= 2:
                                gp.wait_ge(p3t, it - 1)
                            gp.dma_start(
                                wgb[it % 2][:, :].rearrange("p (b i) -> p b i", i=128),
                                d_hwg[it]).then_inc(sm_wg[it % 2], 16)
                            gp.dma_start(
                                wub[it % 2][:, :].rearrange("p (b i) -> p b i", i=128),
                                d_hwu[it]).then_inc(sm_wu[it % 2], 16)
                        for gb in range(2):
                            gp.dma_start(
                                wdb[:, gb * 2048:gb * 2048 + 2048].rearrange(
                                    "p (i hh) -> p i hh", i=4),
                                d_hwd[gb * 4:gb * 4 + 4, :, 0, :, :].rearrange(
                                    "i p a b -> p i (a b)")
                                ).then_inc(sm_wd[gb], 16)

                    @blk.tensor
                    def _(te):
                        for it in range(NIT):
                            par = it % 2
                            te.wait_ge(sm_wg[it % 2], 16 * (it // 2 + 1))
                            te.wait_ge(sm_wu[it % 2], 16 * (it // 2 + 1))
                            if it >= 2:
                                te.wait_ge(p3s, 2 * (it - 2) + 2)
                                te.wait_ge(p3v, 3 * (it - 2) + 2)
                            for w_b, p_x in [(wgb[par], p_g[par]), (wub[par], p_u[par])]:
                                for blk_h in range(32):
                                    A = w_b[:, blk_h * 128:(blk_h + 1) * 128]
                                    for th in range(2):
                                        ins = te.matmul(
                                            p_x[th][:, :], A,
                                            xnT[:, blk_h * T + th * 512:
                                                blk_h * T + th * 512 + 512],
                                            start=(blk_h == 0), stop=(blk_h == 31))
                            ins.then_inc(p3t, 1)

                    @blk.scalar
                    def _(sc):
                        for it in range(NIT):
                            par = it % 2
                            if it >= 1:
                                sc.wait_ge(p3v, 3 * (it - 1) + 3)
                                sc.dma_start(d_hTd[it - 1], hTt[(it - 1) % 2][:, :]
                                             ).then_inc(sm_hT_st[(it - 1) % 2], 16)
                            sc.wait_ge(p3t, it + 1)
                            if it >= 2:
                                sc.wait_ge(p3v, 3 * (it - 2) + 3)
                            for th in range(2):
                                sc.activation(sil[par][:, th * 512:th * 512 + 512],
                                              p_g[par][th][:, :], AF.Silu,
                                              scale=sg_cols[:, it:it + 1]
                                              ).then_inc(p3s, 1)
                        sc.wait_ge(p3v, 3 * (NIT - 1) + 3)
                        sc.dma_start(d_hTd[NIT - 1], hTt[(NIT - 1) % 2][:, :]
                                     ).then_inc(sm_hT_st[(NIT - 1) % 2], 16)

                    @blk.vector
                    def _(ve):
                        for it in range(NIT):
                            par = it % 2
                            ve.wait_ge(p3t, it + 1)
                            if it >= 2:
                                ve.wait_ge(p3v, 3 * it - 3)
                            for th in range(2):
                                ve.tensor_scalar(uss[par][:, th * 512:th * 512 + 512],
                                                 p_u[par][th][:, :],
                                                 su_cols[:, it:it + 1], None,
                                                 ALU.mult).then_inc(p3v, 1)
                            ve.wait_ge(p3s, 2 * it + 2)
                            if it >= 2:
                                ve.wait_ge(sm_hT_st[it % 2], 16 * (it // 2))
                            ve.wait_ge(p3v, 3 * it + 2)
                            ve.tensor_tensor(hTt[par][:, :], sil[par][:, :],
                                             uss[par][:, :], ALU.mult).then_inc(p3v, 1)

        with ExitStack() as s4:
            sb4 = lambda n, s, d: s4.enter_context(nc.sbuf_tensor(n, s, d))
            ps4 = lambda n, s, d: s4.enter_context(nc.psum_tensor(n, s, d))
            hTr = sb4("hTr", [128, 48 * 1024], BF16)
            hTl = [sb4(f"hTl{i}", [128, 2048], BF16) for i in range(4)]
            gTs = [sb4(f"gTs{i}", [128, T], F32) for i in range(4)]
            sTl = [sb4(f"sTl{i}", [128, T], F32) for i in range(4)]
            xTl = [sb4(f"xTl{i}", [128, T], F32) for i in range(4)]
            outs = [sb4(f"outs{i}", [128, T], F32) for i in range(4)]
            tmpm = sb4("tmpm", [128, T], F32)
            p_acc = [[ps4(f"p_acc{h}_{t}", [128, 512], F32) for t in range(2)]
                     for h in range(4)]

            NB = 22
            NPAIR = 19
            SEQ = P4SEQ
            assert sorted(SEQ) == list(range(NIT))
            POS = {it: j for j, it in enumerate(SEQ)}
            ht_use = {}
            _cnt4 = [0, 0, 0, 0]
            for _g in range(8):
                for _p in range(NPAIR):
                    _cnt4[_p % 4] += 1
                    ht_use[(_g, _p)] = _cnt4[_p % 4]
            wd_use = {}
            _cnt4w = [0, 0, 0, 0]
            for _g in range(8):
                for _b in range(NB):
                    _gb = _g * NB + _b
                    _cnt4w[_gb % 4] += 1
                    wd_use[(_g, _b)] = _cnt4w[_gb % 4]

            with nc.Block("down") as blk:
                @blk.sync
                def _(sy):
                    for r in (0, 2, 4):
                        sy.wait_ge(sm_hT_st[0], 16 * (4 * r + 4))
                        sy.wait_ge(sm_hT_st[1], 16 * (4 * r + 4))
                        sy.dma_start(
                            hTr[:, r * 8192:(r + 1) * 8192].rearrange(
                                "q (i t) -> q i t", i=8),
                            d_hTd[8 * r:8 * r + 8, :, :].rearrange(
                                "i q t -> q i t")).then_inc(sm_hr[r], 16)
                    sy.wait_ge(sm_hT_st[0], 16 * 43)
                    sy.wait_ge(sm_hT_st[1], 16 * 43)
                    sy.wait_ge(sm_sT_st[0], 16 * 16)
                    sy.wait_ge(sm_sT_st[1], 16 * 16)
                    for g in range(8):
                        for p in range(NPAIR):
                            if p >= 4:
                                sy.wait_ge(p4t,
                                           g * NIT + POS[48 + 2 * (p - 4) + 1] + 1)
                            elif g >= 1:
                                q = {0: 16, 1: 17, 2: 18, 3: 15}[p]
                                sy.wait_ge(p4t,
                                           (g - 1) * NIT + POS[48 + 2 * q + 1] + 1)
                            sy.dma_start(
                                hTl[p % 4][:, :].rearrange(
                                    "q (i t) -> q i t", i=2),
                                d_hTd[48 + 2 * p:48 + 2 * p + 2, :, :].rearrange(
                                    "i q t -> q i t")).then_inc(sm_hT_ld[p % 4], 16)
                            if p == 2:
                                for ht in range(4):
                                    hg = g * 4 + ht
                                    if g >= 1:
                                        sy.wait_ge(p4v, 8 * (g - 1) + 2 * (ht + 1))
                                    sy.dma_start(xTl[ht][:, :],
                                                 d_xT[hg * 128:(hg + 1) * 128, :]
                                                 ).then_inc(sm_xT[ht], 16)
                    for ht in range(4):
                        sy.wait_ge(sm_out[ht], 16 * 8)

                @blk.gpsimd
                def _(gp):
                    for g in range(8):
                        for b in range(NB):
                            it0 = b * 4
                            nit = min(4, NIT - it0)
                            gb = g * NB + b
                            if gb < 2:
                                continue
                            if gb >= 4:
                                pgb = gb - 4
                                pg, pb = pgb // NB, pgb % NB
                                gp.wait_ge(p4t,
                                           pg * NIT + min(4 * (pb + 1), NIT))
                            sl = (gb % 4) * 2048
                            dst = wdb[:, sl:sl + nit * 512]
                            gp.dma_start(
                                dst.rearrange("p (i hh) -> p i hh", i=nit),
                                d_hwd[it0:it0 + nit, :, g, :, :].rearrange(
                                    "i p a b -> p i (a b)")
                                ).then_inc(sm_wd[gb % 4], 16)
                        for ht in range(4):
                            hg = g * 4 + ht
                            if g >= 1:
                                gp.wait_ge(p4v, 8 * (g - 1) + 2 * (ht + 1))
                            gp.wait_ge(sm_sT_st[hg % 2], 16 * (hg // 2 + 1))
                            gp.dma_start(sTl[ht][:, :],
                                         d_sTd[hg * 128:(hg + 1) * 128, :]
                                         ).then_inc(sm_sT_ld[ht], 16)

                @blk.tensor
                def _(te):
                    for g in range(8):
                        for j in range(NIT):
                            it = SEQ[j]
                            b = j // 4
                            gb = g * NB + b
                            te.wait_ge(sm_wd[gb % 4], 16 * wd_use[(g, b)])
                            if it < 48:
                                if g == 0:
                                    te.wait_ge(sm_hr[it // 8], 16)
                                slot, sbase = hTr, it * 1024
                            else:
                                pr = (it - 48) // 2
                                te.wait_ge(sm_hT_ld[pr % 4],
                                           16 * ht_use[(g, pr)])
                                slot, sbase = hTl[pr % 4], (it % 2) * 1024
                            base = (gb % 4) * 2048 + (j % 4) * 512
                            for ht in range(4):
                                A = wdb[:, base + ht * 128:base + ht * 128 + 128]
                                for th in range(2):
                                    if j == 0 and g >= 1:
                                        te.wait_ge(p4s,
                                                   8 * (g - 1) + 2 * ht + th + 1)
                                    B = slot[:, sbase + th * 512:
                                             sbase + th * 512 + 512]
                                    ins = te.matmul(p_acc[ht][th][:, :], A, B,
                                                    start=(j == 0), stop=(j == NIT - 1))
                            ins.then_inc(p4t, 1)

                @blk.scalar
                def _(sc):
                    for r in (1, 3, 5):
                        sc.wait_ge(sm_hT_st[0], 16 * (4 * r + 4))
                        sc.wait_ge(sm_hT_st[1], 16 * (4 * r + 4))
                        sc.dma_start(
                            hTr[:, r * 8192:(r + 1) * 8192].rearrange(
                                "q (i t) -> q i t", i=8),
                            d_hTd[8 * r:8 * r + 8, :, :].rearrange(
                                "i q t -> q i t")).then_inc(sm_hr[r], 16)
                    for g in range(8):
                        sc.wait_ge(p4t, g * NIT + NIT)
                        for ht in range(4):
                            if g >= 1:
                                sc.wait_ge(p4v, 8 * (g - 1) + 2 * (ht + 1))
                            for th in range(2):
                                sc.activation(gTs[ht][:, th * 512:th * 512 + 512],
                                              p_acc[ht][th][:, :], AF.Silu,
                                              scale=sd_cols[:, g * 4 + ht:g * 4 + ht + 1]
                                              ).then_inc(p4s, 1)
                        for ht in range(4):
                            hg = g * 4 + ht
                            sc.wait_ge(p4v, 8 * g + 2 * (ht + 1))
                            sc.dma_start(d_out[hg * 128:(hg + 1) * 128, :],
                                         outs[ht][:, :]).then_inc(sm_out[ht], 16)

                @blk.vector
                def _(ve):
                    for g in range(8):
                        for ht in range(4):
                            k = 4 * g + ht
                            ve.wait_ge(p4s, 8 * g + 2 * (ht + 1))
                            ve.wait_ge(sm_xT[ht], 16 * (g + 1))
                            ve.wait_ge(sm_sT_ld[ht], 16 * (g + 1))
                            if g >= 1:
                                ve.wait_ge(sm_out[ht], 16 * g)
                            if k >= 1:
                                ve.wait_ge(p4v, 2 * k)
                            ve.tensor_tensor(tmpm[:, :], gTs[ht][:, :], sTl[ht][:, :],
                                             ALU.mult).then_inc(p4v, 1)
                            ve.wait_ge(p4v, 2 * k + 1)
                            ve.tensor_tensor(outs[ht][:, :], tmpm[:, :], xTl[ht][:, :],
                                             ALU.add).then_inc(p4v, 1)

    nc.finalize()
    return nc


_NC_CACHE = None


def kernel(L0_attn_data, L0_attn_weight, L0_attn_mask, L0_ffn_w_gate, L0_ffn_s_gate,
           L0_ffn_w_up, L0_ffn_s_up, L0_ffn_w_down, L0_ffn_s_down):
    global _NC_CACHE
    asc = np.ascontiguousarray

    x2 = np.asarray(L0_attn_data, dtype=np.float32).reshape(NCORES * T, D)
    w = np.asarray(L0_attn_weight, dtype=np.float32)
    mask = np.asarray(L0_attn_mask, dtype=np.int32)
    wg = np.asarray(L0_ffn_w_gate, dtype=np.int32)
    wu = np.asarray(L0_ffn_w_up, dtype=np.int32)
    wd = np.asarray(L0_ffn_w_down, dtype=np.int32)
    sg = np.asarray(L0_ffn_s_gate, dtype=np.float32)
    su = np.asarray(L0_ffn_s_up, dtype=np.float32)
    sd = np.asarray(L0_ffn_s_down, dtype=np.float32)

    hwt = asc(w.reshape(32, 128, 32, 128).transpose(0, 3, 2, 1))
    hwg = asc(wg.reshape(NIT, 128, 32, 128).transpose(0, 3, 2, 1))
    hwu = asc(wu.reshape(NIT, 128, 32, 128).transpose(0, 3, 2, 1))
    hwd = asc(wd.reshape(8, 4, 128, NIT, 128).transpose(3, 4, 0, 1, 2)[P4SEQ])
    sgv = asc(sg.reshape(NIT, 128))
    suv = asc(su.reshape(NIT, 128))
    sdv = asc(sd.reshape(32, 128))

    c = np.arange(D)
    p = np.arange(128)[:, None]
    e32f = (c // 32 == p).astype(ml_dtypes.bfloat16)
    gmat4 = (np.arange(4) == p // 32).astype(np.float32)
    identb = np.eye(128, dtype=ml_dtypes.bfloat16)
    identf = np.eye(128, dtype=np.float32)

    shared = dict(hwt=hwt, mask=mask, hwg=hwg, hwu=hwu, hwd=hwd, sgv=sgv, suv=suv,
                  sdv=sdv, e32f=e32f, gmat4=gmat4, identb=identb, identf=identf)
    in_maps = []
    for ci in range(NCORES):
        xc = asc(x2[ci * T:(ci + 1) * T])
        in_maps.append(dict(x=xc, xT=asc(xc.T), **shared))

    if _NC_CACHE is None:
        _NC_CACHE = build_nc()
    res = run_bass_kernel_spmd(_NC_CACHE, in_maps, core_ids=list(range(NCORES)))
    out = np.concatenate([np.asarray(r["outT"]).T for r in res.results], axis=0)
    return asc(out.reshape(4, 2048, D).astype(np.float32))
